# version 04
# baseline (speedup 1.0000x reference)
import numpy as np
import ml_dtypes
import concourse.bass as bass
import concourse.mybir as mybir
from concourse.bass_utils import run_bass_kernel_spmd

F32 = mybir.dt.float32
BF16 = mybir.dt.bfloat16
FP8 = mybir.dt.float8e5
AL = mybir.AluOpType
AF = mybir.ActivationFunctionType
AX = mybir.AxisListType
BFNP = ml_dtypes.bfloat16

B, D, DI, S = 4, 1024, 2048, 16
T = 8192
DIH = DI // 2

ENGINES = ("sync", "tensor", "vector", "scalar", "gpsimd")
NO_HALO = False
RAW_SILU = False
NO_CONV = False


class Sem:
    def __init__(self, handle):
        self.h = handle
        self.n = 0


class Prog:
    def __init__(self, nc):
        self.nc = nc
        self.streams = {e: [] for e in ENGINES}

    def wait(self, eng, sem, n=None):
        v = sem.n if n is None else n
        if v <= 0:
            return
        self.streams[eng].append(lambda E, s=sem, v=v: E.wait_ge(s.h, v))

    _PTR_OPS = frozenset((
        "tensor_scalar", "tensor_single_scalar", "tensor_scalar_add",
        "tensor_scalar_sub", "tensor_scalar_mul", "tensor_scalar_max",
        "tensor_scalar_min", "scalar_tensor_tensor", "tensor_tensor_scan",
    ))

    def op(self, eng, meth, *a, inc=None, **kw):
        incs = []
        if inc is not None:
            incs = [inc] if isinstance(inc, tuple) else list(inc)
        for sm, k in incs:
            sm.n += k
        guard = meth in Prog._PTR_OPS and eng in ("vector", "gpsimd")

        def fn(E, meth=meth, a=a, kw=kw, incs=incs, guard=guard):
            if guard:
                E.drain()
            r = getattr(E, meth)(*a, **kw)
            if guard and not incs:
                E.drain()
            if incs:
                r.then_inc(incs[0][0].h, incs[0][1]) if not guard else None
                if guard:
                    E.drain(fusable=False).then_inc(incs[0][0].h, incs[0][1])
                for sm, k in incs[1:]:
                    E.drain(fusable=False).then_inc(sm.h, k)
            return r

        self.streams[eng].append(fn)

    def dma(self, eng, out, in_, sem, **kw):
        self.op(eng, "dma_start", inc=(sem, 16), out=out, in_=in_, **kw)

    def barrier(self, bsem):
        for eng in ENGINES:
            s = bsem[eng]
            s.n += 1
            if eng == "sync":
                self.streams[eng].append(lambda E, s=s: E.sem_inc(s.h, 1))
            elif eng == "tensor":
                self.streams[eng].append(lambda E, s=s: E.nop().then_inc(s.h, 1))
            else:
                self.streams[eng].append(
                    lambda E, s=s: E.drain(fusable=False).then_inc(s.h, 1))
        targets = [(bsem[eng], bsem[eng].n) for eng in ENGINES]
        for eng in ENGINES:
            for s, v in targets:
                self.wait(eng, s, v)

    def finalize(self):
        nc = self.nc
        with nc.Block() as block:
            @block.sync
            def _(E):
                for fn in self.streams["sync"]:
                    fn(E)

            @block.tensor
            def _(E):
                for fn in self.streams["tensor"]:
                    fn(E)

            @block.vector
            def _(E):
                for fn in self.streams["vector"]:
                    fn(E)

            @block.scalar
            def _(E):
                for fn in self.streams["scalar"]:
                    fn(E)

            @block.gpsimd
            def _(E):
                for fn in self.streams["gpsimd"]:
                    fn(E)


class Arena:

    def __init__(self, handle, cols):
        self.t = handle
        self.cols = cols
        self.off = 0

    def alloc(self, n):
        assert self.off + n <= self.cols, f"arena32 overflow {self.off}+{n}"
        v = self.t[:, self.off:self.off + n]
        self.off += n
        return v

    def alloc16(self, n):
        n2 = (n + 1) // 2
        assert self.off + n2 <= self.cols, f"arena16 overflow {self.off}+{n2}"
        v = self.t[:, self.off:self.off + n2].bitcast(BF16)
        self.off += n2
        return v

    def alloc8(self, n):
        n4 = (n + 3) // 4
        assert self.off + n4 <= self.cols, f"arena8 overflow {self.off}+{n4}"
        v = self.t[:, self.off:self.off + n4].bitcast(FP8)
        self.off += n4
        return v

    def mark(self):
        return self.off

    def reset(self, to=0):
        self.off = to


def build_program(dev=(), phases="A"):
    TH = T // 2
    NT = T // 512
    NTH = TH // 512
    TS = 1024 if T >= 1024 else T
    NTS = T // TS
    NH = TS // 512

    nc = bass.Bass("TRN2", target_bir_lowering=False, debug=False)
    dev = set(dev)

    def din(name, shape, dtype=F32):
        return nc.dram_tensor(name, list(shape), dtype, kind="ExternalInput").ap()

    x_in = din("x_tok", (T, D))
    xh_in = din("x_tok_half", (TH, D))
    ipw = din("ipw", (D, 2 * DIH))
    convw_i = din("convw", (DIH, 4))
    convb_i = din("convb", (DIH, 1))
    xpw_i = din("xpw", (DIH, 33))
    dtw_i = din("dtw", (DIH, 1))
    dtb_i = din("dtb", (DIH, 1))
    alog_i = din("alog", (DIH, S))
    dparam_i = din("dparam", (DIH, 1))
    cpwh_i = din("cpwh", (DIH, D))
    ident_i = din("ident", (128, 128))
    w1t_i = din("w1t", (D, 4 * D))
    b1c_i = din("b1c", (128, 32))
    w2t_i = din("w2t", (4 * D, D))
    b2c_i = din("b2c", (128, 8))
    mask_i = din("mask", (D, T // 2), BF16)
    aw1t_i = din("aw1t", (D, D))
    ab1_i = din("ab1", (128, 8))
    wwt_i = din("wwt", (D, 496))
    bw_i = din("bw", (1, 496))
    aw2t_i = din("aw2t", (D, D))
    ab2_i = din("ab2", (128, 8))
    gam_i = din("gam", (128, 8))
    bet_i = din("bet", (128, 8))
    sel_i = din("sel", (4, 128, 64), BF16)
    rsel_i = din("rsel", (8, 4, 64, 128), BF16)
    g1r_i = din("g1r", (4, 4, 8, 32))
    g2r_i = din("g2r", (4, 16, 16))
    g1i_i = din("g1i", (4, 4, 8, 32))
    g2i_i = din("g2i", (4, 16, 16))
    repd_i = din("repd", (8, 128), BF16)
    repg_i = din("repg", (16, 128), BF16)
    brc_i = din("brc", (128, 4))
    bic_i = din("bic", (128, 4))

    out_final = nc.dram_tensor("out_tok", [TH, D], F32, kind="ExternalOutput").ap()

    def scratch(name, shape, dtype):
        return nc.dram_tensor(name, list(shape), dtype).ap()

    xb_hbm = scratch("xb_silu_hbm", (DIH, T), BF16)
    z_hbm = scratch("z_silu_hbm", (DIH, T), BF16)
    cp_cc_in = scratch("cp_cc_in", (2, D, TH), F32)
    cp_cc_out = scratch("cp_cc_out", (D, TH), F32)
    xssm_cc_in = scratch("xssm_cc_in", (33, T), F32)
    xssm_cc_out = scratch("xssm_cc_out", (33, T), F32)
    xssm_bf = scratch("xssm_bf", (32, T), BF16)
    s1_hbm = scratch("s1_hbm", (D, TH), F32)
    s2_hbm = scratch("s2_hbm", (D, TH), F32)
    s3_hbm = scratch("s3_hbm", (D, TH), F32)
    hag_in = scratch("hag_in", (D, TH), BF16)
    hag_out = scratch("hag_out", (4, D, TH), BF16)

    dev_out = {}
    for name in dev:
        shp, dt = {
            "xb_silu": ((DIH, T), BF16), "z_silu": ((DIH, T), BF16),
            "xssm": ((33, T), F32), "cpo": ((D, TH), F32),
            "yg": ((DIH, T), BF16),
            "s2": ((D, TH), F32), "hC": ((D, TH), BF16),
            "wsm": ((TH, 496), F32), "conv": ((D, TH), F32),
            "s3": ((D, TH), F32),
            "s1": ((D, TH), F32),
        }[name]
        dev_out[name] = nc.dram_tensor("dev_" + name, list(shp), dt,
                                       kind="ExternalOutput").ap()

    p = Prog(nc)
    from contextlib import ExitStack
    stack = ExitStack()
    e = stack.enter_context

    sld = Sem(e(nc.semaphore("sld")))
    sg = Sem(e(nc.semaphore("sg")))
    spe = Sem(e(nc.semaphore("spe")))
    sve = Sem(e(nc.semaphore("sve")))
    ssc = Sem(e(nc.semaphore("ssc")))
    sgp = Sem(e(nc.semaphore("sgp")))
    scc = Sem(e(nc.semaphore("scc")))
    bsem = {eng: Sem(e(nc.semaphore("bar_" + eng))) for eng in ENGINES}

    ident_t = e(nc.sbuf_tensor("ident_t", [128, 128], F32))
    const_t = e(nc.sbuf_tensor("const_t", [128, 4], F32))
    for j, cv in enumerate((1e-5, -0.5, 1.0, -1.0)):
        nc.const_aps.register(F32, cv, const_t[:, j:j + 1])
    ones_row = e(nc.sbuf_tensor("ones_row", [1, 128], F32))
    ones_col = e(nc.sbuf_tensor("ones_col", [128, 1], BF16))
    convw_t = e(nc.sbuf_tensor("convw_t", [128, 32], F32))
    convb_t = e(nc.sbuf_tensor("convb_t", [128, 8], F32))
    dtw_t = e(nc.sbuf_tensor("dtw_t", [128, 8], F32))
    dtb_t = e(nc.sbuf_tensor("dtb_t", [128, 8], F32))
    aneg_t = e(nc.sbuf_tensor("aneg_t", [128, 8 * S], F32))
    dparam_t = e(nc.sbuf_tensor("dparam_t", [128, 8], F32))
    state_t = e(nc.sbuf_tensor("state_t", [128, 8 * S], BF16))
    xpw_t = e(nc.sbuf_tensor("xpw_t", [128, 8 * 33], BF16))

    AR = Arena(e(nc.sbuf_tensor("arena", [128, 52352], F32)), 52352)

    ps = [e(nc.psum_tensor(f"ps{i}", [128, 512], F32)) for i in range(8)]
    psm = [Sem(e(nc.semaphore(f"psm{i}"))) for i in range(8)]
    psf = [Sem(e(nc.semaphore(f"psf{i}"))) for i in range(8)]

    p.dma("sync", ident_t[:, :], ident_i[:, :], sld)
    p.wait("vector", sld)
    for j, cv in enumerate((1e-5, -0.5, 1.0, -1.0)):
        p.op("vector", "memset", const_t[:, j:j + 1], cv)
    p.op("vector", "memset", ones_row[:, :], 1.0)
    p.op("vector", "memset", ones_col[:, :], 1.0)
    p.op("vector", "memset", state_t[:, :], 0.0, inc=(sve, 1))

    ipw_t = AR.alloc16(8 * 2048)
    stage = AR.alloc(2048)
    for kb in range(8):
        p.wait("sync", ssc)
        p.dma("sync", stage[:, :], ipw[kb * 128:(kb + 1) * 128, :], sld)
        p.wait("scalar", sld)
        p.op("scalar", "copy", out=ipw_t[:, kb * 2048:(kb + 1) * 2048],
             in_=stage[:, :], inc=(ssc, 1))
    for db in range(8):
        rs = slice(db * 128, (db + 1) * 128)
        p.dma("sync", convw_t[:, db * 4:(db + 1) * 4], convw_i[rs, :], sld)
        p.dma("sync", convb_t[:, db:db + 1], convb_i[rs, :], sld)
        p.dma("sync", dtw_t[:, db:db + 1], dtw_i[rs, :], sld)
        p.dma("sync", dtb_t[:, db:db + 1], dtb_i[rs, :], sld)
        p.dma("sync", aneg_t[:, db * S:(db + 1) * S], alog_i[rs, :], sld)
        p.dma("sync", dparam_t[:, db:db + 1], dparam_i[rs, :], sld)
    p.wait("sync", ssc)
    for db in range(8):
        p.dma("sync", stage[:, db * 33:db * 33 + 33],
              xpw_i[db * 128:(db + 1) * 128, :], sld)
    p.wait("scalar", sld)
    p.op("scalar", "copy", out=xpw_t[:, :], in_=stage[:, 0:8 * 33])
    p.op("scalar", "activation", out=aneg_t[:, :], in_=aneg_t[:, :], func=AF.Exp,
         inc=(ssc, 1))
    p.wait("vector", ssc)
    p.op("vector", "tensor_scalar_mul", out=aneg_t[:, :], in0=aneg_t[:, :],
         scalar1=-1.0, inc=(sve, 1))
    a0_sc = ssc.n
    AR.reset(8 * 1024)

    xtok = AR.alloc(2 * 4096)
    xbbuf = AR.alloc(2 * 8 * 520)
    cvo = AR.alloc(2 * 512)
    xsst = AR.alloc(512)
    xfm = AR.alloc16(2 * 8 * 512)
    xbs = AR.alloc16(8 * 512)
    zst = AR.alloc16(8 * 512)

    def xbB(par, db):
        off = par * 8 * 520 + db * 520
        return xbbuf[:, off:off + 520]

    p.wait("gpsimd", sve)
    p.op("gpsimd", "memset", xbbuf[:, 0:2 * 8 * 520], 0.0, inc=(sgp, 1))
    halo_init = sgp.n
    p.wait("vector", sgp, halo_init)

    trans_done = {}
    conv_done = {}
    halo_c = {}
    store_done = {}
    xproj_done = {}
    xssm_ev = {}
    xssm_dma = {}
    silu_c = {}

    for it in range(NT):
        par = it % 2
        tsl = slice(it * 512, (it + 1) * 512)
        xt = xtok[:, par * 4096:(par + 1) * 4096]
        xfm_p = xfm[:, par * 8 * 512:(par + 1) * 8 * 512]

        if it == 0:
            p.wait("sync", ssc, a0_sc)
        if it >= 2:
            p.wait("sync", spe, trans_done[it - 2])
        for tt in range(4):
            p.dma("sync", xt[:, tt * 1024:(tt + 1) * 1024],
                  x_in[it * 512 + tt * 128: it * 512 + (tt + 1) * 128, :], sld)
        ld_all = sld.n

        p.wait("tensor", sld, ld_all)
        for db in range(8):
            bank = db % 2
            p.wait("tensor", psf[bank])
            for tt in range(4):
                p.op("tensor", "transpose", ps[bank][:, tt * 128:(tt + 1) * 128],
                     xt[:, tt * 1024 + db * 128: tt * 1024 + (db + 1) * 128],
                     ident_t[:, :], inc=(psm[bank], 1) if tt == 3 else None)
            p.wait("scalar", psm[bank])
            p.op("scalar", "copy", out=xfm_p[:, db * 512:(db + 1) * 512],
                 in_=ps[bank][:, :],
                 inc=[(psf[bank], 1)] + ([(ssc, 1)] if db == 7 else []))
        p.op("tensor", "nop", inc=(spe, 1))
        trans_done[it] = spe.n
        xfm_ready = ssc.n

        p.wait("tensor", ssc, xfm_ready)
        for mb in range(16):
            bank = 2 + (mb % 5)
            p.wait("tensor", psf[bank])
            for kb in range(8):
                p.op("tensor", "matmul", ps[bank][:, :],
                     ipw_t[:, kb * 2048 + mb * 128: kb * 2048 + (mb + 1) * 128],
                     xfm_p[:, kb * 512:(kb + 1) * 512],
                     start=(kb == 0), stop=(kb == 7),
                     inc=(psm[bank], 1) if kb == 7 else None)
            if mb < 8:
                db = mb
                if it >= 2 and mb == 0:
                    p.wait("vector", sve, conv_done[it - 2])
                p.wait("vector", psm[bank])
                p.op("vector", "tensor_copy", out=xbB(par, db)[:, 3:515],
                     in_=ps[bank][:, :], inc=[(psf[bank], 1), (sve, 1)])
                xb = xbB(par, db)
                if not NO_HALO:
                    p.wait("gpsimd", sve)
                    if it == 0 and db == 0:
                        p.wait("gpsimd", sgp, halo_init)
                    p.op("gpsimd", "tensor_copy", out=xbB(1 - par, db)[:, 0:3],
                         in_=xb[:, 512:515], inc=(sgp, 1))
                    halo_c[(it, db)] = sgp.n
                cv = cvo[:, (db % 2) * 512:(db % 2 + 1) * 512]
                prev = silu_c.get((it, db - 2)) or silu_c.get((it - 1, db + 6))
                if prev:
                    p.wait("vector", ssc, prev)
                hprev = halo_c.get((it - 1, db))
                if hprev and not NO_HALO:
                    p.wait("vector", sgp, hprev)
                if NO_CONV:
                    p.op("vector", "tensor_copy", out=cv[:, :], in_=xb[:, 3:515],
                         inc=(sve, 1))
                else:
                    p.op("vector", "tensor_scalar", out=cv[:, :], in0=xb[:, 0:512],
                         scalar1=convw_t[:, db * 4:db * 4 + 1], scalar2=None,
                         op0=AL.mult)
                    for k in (1, 2, 3):
                        p.op("vector", "scalar_tensor_tensor", out=cv[:, :],
                             in0=xb[:, k:k + 512],
                             scalar=convw_t[:, db * 4 + k:db * 4 + k + 1],
                             in1=cv[:, :], op0=AL.mult, op1=AL.add,
                             inc=(sve, 1) if k == 3 else None)
                if it >= 1 and db == 0:
                    p.wait("scalar", spe, xproj_done[it - 1])
                    p.wait("scalar", sld, store_done[it - 1])
                p.wait("scalar", sve)
                p.op("scalar", "activation", out=xbs[:, db * 512:(db + 1) * 512],
                     in_=(cv[:, :] if not RAW_SILU else xb[:, 3:515]),
                     func=AF.Silu, bias=convb_t[:, db:db + 1],
                     inc=(ssc, 1))
                silu_c[(it, db)] = ssc.n
            else:
                if it >= 1 and mb == 8:
                    p.wait("scalar", sld, store_done[it - 1])
                p.wait("scalar", psm[bank])
                p.op("scalar", "activation",
                     out=zst[:, (mb - 8) * 512:(mb - 7) * 512],
                     in_=ps[bank][:, :], func=AF.Silu,
                     inc=[(psf[bank], 1)] + ([(ssc, 1)] if mb == 15 else []))
        conv_done[it] = sve.n
        tile_sc = ssc.n

        p.wait("tensor", ssc, tile_sc)
        p.wait("tensor", psf[7])
        for db in range(8):
            p.op("tensor", "matmul", ps[7][0:33, :],
                 xpw_t[:, db * 33:(db + 1) * 33],
                 xbs[:, db * 512:(db + 1) * 512],
                 start=(db == 0), stop=(db == 7),
                 inc=(psm[7], 1) if db == 7 else None)
        p.op("tensor", "nop", inc=(spe, 1))
        xproj_done[it] = spe.n
        p.wait("vector", psm[7])
        if it >= 1:
            p.wait("vector", sld, xssm_dma[it - 1])
        p.op("vector", "tensor_copy", out=xsst[0:33, :],
             in_=ps[7][0:33, :], inc=[(psf[7], 1), (sve, 1)])
        xssm_ev[it] = sve.n
        p.wait("sync", sve, xssm_ev[it])
        p.dma("sync", xssm_cc_in[:, tsl], xsst[0:33, :], sld)
        xssm_dma[it] = sld.n

        p.wait("sync", ssc, tile_sc)
        p.dma("sync", xb_hbm[:, tsl].rearrange("(a p) t -> p a t", p=128),
              xbs[:, :].rearrange("p (a t) -> p a t", a=8), sld)
        p.dma("sync", z_hbm[:, tsl].rearrange("(a p) t -> p a t", p=128),
              zst[:, :].rearrange("p (a t) -> p a t", a=8), sld)
        store_done[it] = sld.n

    all_stores = sld.n
    if "xb_silu" in dev_out:
        p.wait("sync", sld, all_stores)
        p.dma("sync", dev_out["xb_silu"][:, :], xb_hbm[:, :], sld)
    if "z_silu" in dev_out:
        p.wait("sync", sld, all_stores)
        p.dma("sync", dev_out["z_silu"][:, :], z_hbm[:, :], sld)

    p.wait("gpsimd", sld, all_stores)
    p.op("gpsimd", "collective_compute", "AllReduce", AL.add,
         replica_groups=[[0, 1], [2, 3], [4, 5], [6, 7]],
         ins=[xssm_cc_in[:, :].opt()], outs=[xssm_cc_out[:, :].opt()],
         inc=(scc, 1))
    xssm_cc_done = scc.n
    if "xssm" in dev_out:
        p.wait("gpsimd", scc, xssm_cc_done)
        for rr in range(33):
            p.dma("gpsimd", dev_out["xssm"][rr:rr + 1, :],
                  xssm_cc_out[rr:rr + 1, :], sg)
        p.wait("gpsimd", sg)

    p.barrier(bsem)
    AR.reset(0)
    cpwh_t = AR.alloc16(8 * 1024)
    dtrow_bc = AR.alloc(TS)
    dt_all = AR.alloc(8 * TS)
    dtx_all = AR.alloc16(8 * TS)
    a_t = AR.alloc16(2 * TS)
    u_t = AR.alloc16(TS)
    sp_tmp = AR.alloc(TS)
    h_t = AR.alloc16(2 * TS)
    ym_t = dtrow_bc[:, 0:TS // 2].bitcast(BF16)
    y_all = AR.alloc16(8 * TS)
    cp_st = AR.alloc(2 * 512)
    bbc = AR.alloc16(2 * TS)
    cbc = AR.alloc16(2 * TS)
    xbs_rd = AR.alloc16(2 * 8 * TS)
    zs_rd = AR.alloc16(8 * TS)

    xst32 = dt_all
    xst16 = dtx_all
    p.wait("sync", scc, xssm_cc_done)
    p.dma("sync", xst32[0:32, 0:T], xssm_cc_out[0:32, :], sld)
    p.wait("scalar", sld)
    p.op("scalar", "copy", out=xst16[0:32, 0:T], in_=xst32[0:32, 0:T],
         inc=(ssc, 1))
    p.wait("sync", ssc)
    p.dma("sync", xssm_bf[0:32, :], xst16[0:32, 0:T], sld)
    xbf_done = sld.n

    cstage = zs_rd[:, 0:2048].bitcast(F32)
    for kb in range(8):
        p.wait("sync", ssc)
        p.dma("sync", cstage[:, :], cpwh_i[kb * 128:(kb + 1) * 128, :], sld)
        p.wait("scalar", sld)
        p.op("scalar", "copy", out=cpwh_t[:, kb * 1024:(kb + 1) * 1024],
             in_=cstage[:, :], inc=(ssc, 1))
    cpw_ready = ssc.n

    grp_done = {}
    sub_done = {}
    yg_done = {}
    state_c = {}
    cp_dma = {}
    xssm_ld = {}

    for ts_i in range(NTS):
        half = (ts_i * TS) // TH
        tsl = slice(ts_i * TS, (ts_i + 1) * TS)
        lsl = slice(ts_i * TS - half * TH, (ts_i + 1) * TS - half * TH)
        par = ts_i % 2
        xbr = xbs_rd[:, par * 8 * TS:(par + 1) * 8 * TS]

        if ts_i >= 2:
            p.wait("sync", spe, sub_done[ts_i - 2][1])
            p.wait("sync", sve, sub_done[ts_i - 2][0])
        p.wait("sync", sld, all_stores)
        p.dma("sync", xbr[:, :].rearrange("p (a t) -> p a t", a=8),
              xb_hbm[:, tsl].rearrange("(a p) t -> p a t", p=128), sld)
        if ts_i >= 1:
            p.wait("sync", sve, sub_done[ts_i - 1][0])
        p.wait("sync", scc, xssm_cc_done)
        p.dma("sync", dtrow_bc[:, :],
              xssm_cc_out[32:33, tsl].partition_broadcast(128)[:, 0, :], sld)
        ld_sub = sld.n

        p.wait("scalar", sld, ld_sub)
        if ts_i == 0:
            p.wait("vector", sld, xbf_done)
        if ts_i >= 1:
            p.wait("scalar", sve, sub_done[ts_i - 1][0])
        p.wait("scalar", ssc, cpw_ready)
        for db in range(8):
            dta = dt_all[:, db * TS:(db + 1) * TS]
            p.op("scalar", "activation", out=sp_tmp[:, :], in_=dtrow_bc[:, :],
                 func=AF.Abs, scale=dtw_t[:, db:db + 1], bias=dtb_t[:, db:db + 1])
            p.op("scalar", "activation", out=sp_tmp[:, :], in_=sp_tmp[:, :],
                 func=AF.Exp, scale=-1.0)
            p.op("scalar", "activation", out=sp_tmp[:, :], in_=sp_tmp[:, :],
                 func=AF.Ln, bias=1.0)
            p.op("scalar", "activation", out=dta[:, :], in_=dtrow_bc[:, :],
                 func=AF.Relu, scale=dtw_t[:, db:db + 1], bias=dtb_t[:, db:db + 1],
                 inc=(ssc, 1))
            p.wait("vector", ssc)
            p.op("vector", "tensor_add", out=dta[:, :], in0=dta[:, :],
                 in1=sp_tmp[:, :])
            p.op("vector", "tensor_mul", out=dtx_all[:, db * TS:(db + 1) * TS],
                 in0=dta[:, :], in1=xbr[:, db * TS:(db + 1) * TS], inc=(sve, 1))
            p.wait("scalar", sve)
        dt_sc = ssc.n

        p.wait("sync", ssc, cpw_ready)
        p.dma("sync", zs_rd[:, :].rearrange("p (a t) -> p a t", a=8),
              z_hbm[:, tsl].rearrange("(a p) t -> p a t", p=128), sld)
        z_ld = sld.n

        for s in range(S):
            spar = s % 2
            bb = bbc[:, spar * TS:(spar + 1) * TS]
            cb = cbc[:, spar * TS:(spar + 1) * TS]
            if (ts_i, (s - 2) * 8 + 7) in grp_done:
                p.wait("sync", sve, grp_done[(ts_i, (s - 2) * 8 + 7)])
                p.wait("sync", sgp, state_c[(ts_i, (s - 2) * 8 + 7)])
            elif s < 2 and ts_i >= 1:
                p.wait("sync", sve, grp_done[(ts_i - 1, (S - 2 + s) * 8 + 7)])
                p.wait("sync", sgp, state_c[(ts_i - 1, (S - 2 + s) * 8 + 7)])
            if ts_i == 0 and s == 0:
                p.wait("sync", sld, xbf_done)
            p.dma("sync", bb[:, :],
                  xssm_bf[s:s + 1, tsl].partition_broadcast(128)[:, 0, :], sld)
            p.dma("sync", cb[:, :],
                  xssm_bf[16 + s:17 + s, tsl].partition_broadcast(128)[:, 0, :],
                  sld)
            bc_ld = sld.n

            for db in range(8):
                g = s * 8 + db
                apar = g % 2
                at = a_t[:, apar * TS:(apar + 1) * TS]
                if (ts_i, g - 2) in grp_done:
                    p.wait("scalar", sve, grp_done[(ts_i, g - 2)])
                elif g < 2 and ts_i >= 1:
                    p.wait("scalar", sve, grp_done[(ts_i - 1, S * 8 - 2 + g)])
                p.wait("scalar", ssc, dt_sc)
                p.op("scalar", "activation", out=at[:, :],
                     in_=dt_all[:, db * TS:(db + 1) * TS], func=AF.Exp,
                     scale=aneg_t[:, db * S + s:db * S + s + 1], inc=(ssc, 1))
                a_rdy = ssc.n

                p.wait("vector", sld, bc_ld)
                p.op("vector", "tensor_tensor", out=u_t[:, :],
                     in0=dtx_all[:, db * TS:(db + 1) * TS], in1=bb[:, :],
                     op=AL.mult)
                p.wait("vector", ssc, a_rdy)
                ht = h_t[:, (g % 2) * TS:(g % 2 + 1) * TS]
                if ts_i >= 1:
                    p.wait("vector", sgp, state_c[(ts_i - 1, g)])
                hbuf_prev = state_c.get((ts_i, g - 2)) or (
                    state_c.get((ts_i - 1, S * 8 - 2 + g)) if ts_i >= 1 else None)
                if hbuf_prev:
                    p.wait("vector", sgp, hbuf_prev)
                p.op("vector", "tensor_tensor_scan", out=ht[:, :],
                     data0=at[:, :], data1=u_t[:, :],
                     initial=state_t[:, db * S + s:db * S + s + 1],
                     op0=AL.mult, op1=AL.add, inc=(sve, 1))
                grp_done[(ts_i, g)] = sve.n
                p.wait("gpsimd", sve)
                if g == 0 and ts_i >= 1:
                    p.wait("gpsimd", sve, yg_done[ts_i - 1])
                p.op("gpsimd", "tensor_copy",
                     out=state_t[:, db * S + s:db * S + s + 1],
                     in_=ht[:, TS - 1:TS])
                p.op("gpsimd", "tensor_tensor", out=ym_t[:, :],
                     in0=ht[:, :], in1=cb[:, :], op=AL.mult)
                ya = y_all[:, db * TS:(db + 1) * TS]
                if s == 0:
                    p.op("gpsimd", "tensor_copy", out=ya[:, :], in_=ym_t[:, :],
                         inc=(sgp, 1))
                else:
                    p.op("gpsimd", "tensor_tensor", out=ya[:, :], in0=ya[:, :],
                         in1=ym_t[:, :], op=AL.add, inc=(sgp, 1))
                state_c[(ts_i, g)] = sgp.n

        p.wait("vector", sgp, state_c[(ts_i, S * 8 - 1)])
        p.wait("vector", sld, z_ld)
        for db in range(8):
            ya = y_all[:, db * TS:(db + 1) * TS]
            p.op("vector", "scalar_tensor_tensor", out=ya[:, :],
                 in0=xbr[:, db * TS:(db + 1) * TS],
                 scalar=dparam_t[:, db:db + 1], in1=ya[:, :],
                 op0=AL.mult, op1=AL.add)
            p.op("vector", "tensor_tensor", out=xbr[:, db * TS:(db + 1) * TS],
                 in0=ya[:, :], in1=zs_rd[:, db * TS:(db + 1) * TS], op=AL.mult,
                 inc=(sve, 1))
        yg_rdy = sve.n
        yg_done[ts_i] = yg_rdy

        p.wait("tensor", sve, yg_rdy)
        chunk = ts_i // max(NTS // 2, 1)
        ccol = (ts_i % max(NTS // 2, 1)) * TS
        slot_dma = {}
        for mb in range(8):
            for hh in range(NH):
                u = mb * NH + hh
                bank = u % 4
                slot = u % 2
                p.wait("tensor", psf[bank])
                for kb in range(8):
                    p.op("tensor", "matmul", ps[bank][:, :],
                         cpwh_t[:, kb * 1024 + mb * 128: kb * 1024 + (mb + 1) * 128],
                         xbr[:, kb * TS + hh * 512: kb * TS + (hh + 1) * 512],
                         start=(kb == 0), stop=(kb == 7),
                         inc=(psm[bank], 1) if kb == 7 else None)
                prev = slot_dma.get(u - 2) or (cp_dma.get(ts_i - 1) if u < 2 else None)
                if prev:
                    p.wait("scalar", sld, prev)
                p.wait("scalar", psm[bank])
                p.op("scalar", "copy", out=cp_st[:, slot * 512:(slot + 1) * 512],
                     in_=ps[bank][:, :], inc=[(psf[bank], 1), (ssc, 1)])
                p.wait("sync", ssc)
                p.dma("sync",
                      cp_cc_in[chunk, mb * 128:(mb + 1) * 128,
                               ccol + hh * 512: ccol + (hh + 1) * 512],
                      cp_st[:, slot * 512:(slot + 1) * 512], sld)
                slot_dma[u] = sld.n
        p.op("tensor", "nop", inc=(spe, 1))
        sub_done[ts_i] = (sve.n, spe.n)
        cp_dma[ts_i] = sld.n
        if "yg" in dev_out:
            p.wait("gpsimd", sve, yg_rdy)
            p.dma("gpsimd",
                  dev_out["yg"][:, tsl].rearrange("(a p) t -> p a t", p=128),
                  xbr[:, :].rearrange("p (a t) -> p a t", a=8), sg)

    p.wait("gpsimd", sld, sld.n)
    p.op("gpsimd", "collective_compute", "ReduceScatter", AL.add,
         replica_groups=[[0, 1], [2, 3], [4, 5], [6, 7]],
         ins=[cp_cc_in[:, :, :].opt()], outs=[cp_cc_out[:, :].opt()],
         inc=(scc, 1))
    cp_cc_done = scc.n
    if "cpo" in dev_out:
        p.wait("gpsimd", scc, cp_cc_done)
        p.dma("gpsimd", dev_out["cpo"][:, :], cp_cc_out[:, :], sg)

    p.barrier(bsem)
    AR.reset(0)
    xtok3 = AR.alloc(2 * 4096)
    cp_rd = AR.alloc(2 * 8 * 512)
    mo_t = AR.alloc(2 * 8 * 512)
    s1out = AR.alloc(8 * 512)
    tokout = AR.alloc(2 * 1024)
    lnr = AR.alloc(2048)
    lnr1 = lnr[0:1, :]
    s1bf = AR.alloc16(8 * 512)
    sq16 = AR.alloc16(8 * 512)

    a3_done = {}
    tok_dma = {}
    for nt in range(NTH):
        par = nt % 2
        tsl = slice(nt * 512, (nt + 1) * 512)
        xt3 = xtok3[:, par * 4096:(par + 1) * 4096]
        cpr = cp_rd[:, par * 4096:(par + 1) * 4096]
        mo = mo_t[:, par * 4096:(par + 1) * 4096]

        p.wait("sync", scc, cp_cc_done)
        if nt >= 2:
            p.wait("sync", sve, a3_done[nt - 2][0])
            p.wait("sync", spe, a3_done[nt - 2][1])
        p.dma("sync", cpr[:, :].rearrange("p (a t) -> p a t", a=8),
              cp_cc_out[:, tsl].rearrange("(a p) t -> p a t", p=128), sld)
        for tt in range(4):
            p.dma("sync", xt3[:, tt * 1024:(tt + 1) * 1024],
                  xh_in[nt * 512 + tt * 128: nt * 512 + (tt + 1) * 128, :], sld)
        xld = sld.n

        p.wait("tensor", sld, xld)
        for mb in range(8):
            tbank = mb % 2
            p.wait("tensor", psf[tbank])
            for tt in range(4):
                p.op("tensor", "transpose",
                     ps[tbank][:, tt * 128:(tt + 1) * 128],
                     xt3[:, tt * 1024 + mb * 128: tt * 1024 + (mb + 1) * 128],
                     ident_t[:, :], inc=(psm[tbank], 1) if tt == 3 else None)
            p.wait("vector", psm[tbank])
            p.op("vector", "tensor_add", out=mo[:, mb * 512:(mb + 1) * 512],
                 in0=cpr[:, mb * 512:(mb + 1) * 512], in1=ps[tbank][:, :],
                 inc=(psf[tbank], 1))
            p.op("vector", "tensor_copy", out=s1bf[:, mb * 512:(mb + 1) * 512],
                 in_=mo[:, mb * 512:(mb + 1) * 512], inc=(sve, 1))
            p.wait("scalar", sve)
            p.op("scalar", "activation", out=sq16[:, mb * 512:(mb + 1) * 512],
                 in_=mo[:, mb * 512:(mb + 1) * 512], func=AF.Square, inc=(ssc, 1))
        p.op("tensor", "nop", inc=(spe, 1))
        sbf_rdy, sq_rdy = sve.n, ssc.n

        p.wait("tensor", sve, sbf_rdy)
        p.wait("tensor", ssc, sq_rdy)
        p.wait("tensor", psf[6])
        p.wait("tensor", psf[7])
        for kb in range(8):
            p.op("tensor", "matmul", ps[6][0:1, :], ones_col[:, :],
                 s1bf[:, kb * 512:(kb + 1) * 512], start=(kb == 0), stop=(kb == 7),
                 inc=(psm[6], 1) if kb == 7 else None)
        for kb in range(8):
            p.op("tensor", "matmul", ps[7][0:1, :], ones_col[:, :],
                 sq16[:, kb * 512:(kb + 1) * 512], start=(kb == 0), stop=(kb == 7),
                 inc=(psm[7], 1) if kb == 7 else None)
        p.wait("vector", psm[6])
        p.wait("vector", psm[7])
        if nt >= 1:
            p.wait("vector", sld, a3_done[nt - 1][2])
        p.op("vector", "tensor_scalar_mul", out=lnr1[:, 0:512],
             in0=ps[6][0:1, :], scalar1=1.0 / D, inc=(psf[6], 1))
        p.op("vector", "tensor_scalar_mul", out=lnr1[:, 512:1024],
             in0=ps[7][0:1, :], scalar1=1.0 / D, inc=(psf[7], 1))
        p.op("vector", "tensor_tensor", out=lnr1[:, 1024:1536],
             in0=lnr1[:, 0:512], in1=lnr1[:, 0:512], op=AL.mult)
        p.op("vector", "tensor_tensor", out=lnr1[:, 512:1024],
             in0=lnr1[:, 512:1024], in1=lnr1[:, 1024:1536], op=AL.subtract,
             inc=(sve, 1))
        p.wait("scalar", sve)
        p.op("scalar", "activation", out=lnr1[:, 512:1024],
             in_=lnr1[:, 512:1024], func=AF.Ln, bias=1e-5)
        p.op("scalar", "activation", out=lnr1[:, 512:1024],
             in_=lnr1[:, 512:1024], func=AF.Exp, scale=-0.5, inc=(ssc, 1))
        p.wait("vector", ssc)
        p.op("vector", "tensor_tensor", out=lnr1[:, 0:512], in0=lnr1[:, 0:512],
             in1=lnr1[:, 512:1024], op=AL.mult)
        p.op("vector", "tensor_scalar_mul", out=lnr1[:, 0:512],
             in0=lnr1[:, 0:512], scalar1=-1.0, inc=(sve, 1))
        ab_rdy = sve.n

        p.wait("tensor", sve, ab_rdy)
        p.wait("tensor", psf[2])
        p.op("tensor", "matmul", ps[2][:, :], ones_row[:, :],
             lnr1[0:1, 512:1024], start=True, stop=True, inc=(psm[2], 1))
        p.wait("tensor", psf[3])
        p.op("tensor", "matmul", ps[3][:, :], ones_row[:, :],
             lnr1[0:1, 0:512], start=True, stop=True, inc=(psm[3], 1))
        p.wait("vector", psm[2])
        p.wait("vector", psm[3])
        for db in range(8):
            p.op("vector", "tensor_tensor", out=s1out[:, db * 512:(db + 1) * 512],
                 in0=mo[:, db * 512:(db + 1) * 512], in1=ps[2][:, :],
                 op=AL.mult)
            p.op("vector", "tensor_add", out=s1out[:, db * 512:(db + 1) * 512],
                 in0=s1out[:, db * 512:(db + 1) * 512], in1=ps[3][:, :],
                 inc=[(sve, 1)] + ([(psf[2], 1), (psf[3], 1)] if db == 7 else []))
        s1_rdy = sve.n
        p.wait("sync", sve, s1_rdy)
        p.dma("sync", s1_hbm[:, tsl].rearrange("(a p) t -> p a t", p=128),
              s1out[:, :].rearrange("p (a t) -> p a t", a=8), sld)

        if phases == "A":
            p.wait("tensor", sve, s1_rdy)
            for tt in range(4):
                tb = tt % 2
                tko = tokout[:, tb * 1024:(tb + 1) * 1024]
                for db in range(8):
                    bk = 4 + (db // 4)
                    if db % 4 == 0:
                        p.wait("tensor", psf[bk])
                    p.op("tensor", "transpose",
                         ps[bk][:, (db % 4) * 128:(db % 4 + 1) * 128],
                         s1out[:, db * 512 + tt * 128: db * 512 + (tt + 1) * 128],
                         ident_t[:, :],
                         inc=(psm[bk], 1) if db % 4 == 3 else None)
                if (nt, tt - 2) in tok_dma:
                    p.wait("vector", sld, tok_dma[(nt, tt - 2)])
                elif tt < 2 and (nt - 1, 2 + tt) in tok_dma:
                    p.wait("vector", sld, tok_dma[(nt - 1, 2 + tt)])
                p.wait("vector", psm[4])
                p.op("vector", "tensor_copy", out=tko[:, 0:512], in_=ps[4][:, :],
                     inc=(psf[4], 1))
                p.wait("vector", psm[5])
                p.op("vector", "tensor_copy", out=tko[:, 512:1024], in_=ps[5][:, :],
                     inc=[(psf[5], 1), (sve, 1)])
                p.wait("sync", sve)
                p.dma("sync",
                      out_final[nt * 512 + tt * 128: nt * 512 + (tt + 1) * 128, :],
                      tko[:, :], sld)
                tok_dma[(nt, tt)] = sld.n
        a3_done[nt] = (sve.n, spe.n, sld.n)

    if "s1" in dev_out:
        p.wait("sync", sld)
        p.dma("sync", dev_out["s1"][:, :], s1_hbm[:, :], sld)

    def load_weight_cast(dst, src_dram, kblocks, cols, stage, dt_note=None):
        for kb in range(kblocks):
            p.wait("sync", ssc)
            p.dma("sync", stage[:, 0:cols], src_dram[kb * 128:(kb + 1) * 128, :],
                  sld)
            p.wait("scalar", sld)
            p.op("scalar", "copy", out=dst[:, kb * cols:(kb + 1) * cols],
                 in_=stage[:, 0:cols], inc=(ssc, 1))
        return ssc.n

    ln_state = {"lnr_guard": 0}

    def ln_tile(src, out, lnrv, sbfv, sqv, sbk, abk, gam=None, bet=None):
        p.wait("scalar", sve)
        for db in range(8):
            sl = slice(db * 512, (db + 1) * 512)
            p.op("vector", "tensor_copy", out=sbfv[:, sl], in_=src[:, sl],
                 inc=(sve, 1) if db == 7 else None)
            p.op("scalar", "activation", out=sqv[:, sl], in_=src[:, sl],
                 func=AF.Square, inc=(ssc, 1) if db == 7 else None)
        sbf_rdy, sq_rdy = sve.n, ssc.n
        p.wait("tensor", sve, sbf_rdy)
        p.wait("tensor", ssc, sq_rdy)
        p.wait("tensor", psf[sbk[0]])
        for kb in range(8):
            p.op("tensor", "matmul", ps[sbk[0]][0:1, :], ones_col[:, :],
                 sbfv[:, kb * 512:(kb + 1) * 512], start=(kb == 0), stop=(kb == 7),
                 inc=(psm[sbk[0]], 1) if kb == 7 else None)
        p.wait("tensor", psf[sbk[1]])
        for kb in range(8):
            p.op("tensor", "matmul", ps[sbk[1]][0:1, :], ones_col[:, :],
                 sqv[:, kb * 512:(kb + 1) * 512], start=(kb == 0), stop=(kb == 7),
                 inc=(psm[sbk[1]], 1) if kb == 7 else None)
        p.wait("vector", psm[sbk[0]])
        p.wait("vector", psm[sbk[1]])
        if ln_state["lnr_guard"]:
            p.wait("vector", spe, ln_state["lnr_guard"])
        p.op("vector", "tensor_scalar_mul", out=lnrv[:, 0:512],
             in0=ps[sbk[0]][0:1, :], scalar1=1.0 / D, inc=(psf[sbk[0]], 1))
        p.op("vector", "tensor_scalar_mul", out=lnrv[:, 512:1024],
             in0=ps[sbk[1]][0:1, :], scalar1=1.0 / D, inc=(psf[sbk[1]], 1))
        p.op("vector", "tensor_tensor", out=lnrv[:, 1024:1536],
             in0=lnrv[:, 0:512], in1=lnrv[:, 0:512], op=AL.mult)
        p.op("vector", "tensor_tensor", out=lnrv[:, 512:1024],
             in0=lnrv[:, 512:1024], in1=lnrv[:, 1024:1536], op=AL.subtract,
             inc=(sve, 1))
        p.wait("scalar", sve)
        p.op("scalar", "activation", out=lnrv[:, 512:1024],
             in_=lnrv[:, 512:1024], func=AF.Ln, bias=1e-5)
        p.op("scalar", "activation", out=lnrv[:, 512:1024],
             in_=lnrv[:, 512:1024], func=AF.Exp, scale=-0.5, inc=(ssc, 1))
        p.wait("vector", ssc)
        p.op("vector", "tensor_tensor", out=lnrv[:, 0:512], in0=lnrv[:, 0:512],
             in1=lnrv[:, 512:1024], op=AL.mult)
        p.op("vector", "tensor_scalar_mul", out=lnrv[:, 0:512],
             in0=lnrv[:, 0:512], scalar1=-1.0, inc=(sve, 1))
        ab_rdy = sve.n
        p.wait("tensor", sve, ab_rdy)
        p.wait("tensor", psf[abk])
        p.op("tensor", "matmul", ps[abk][:, :], ones_row[:, :],
             lnrv[0:1, 512:1024], start=True, stop=True, inc=(psm[abk], 1))
        p.wait("vector", psm[abk])
        for db in range(8):
            sl = slice(db * 512, (db + 1) * 512)
            p.op("vector", "tensor_tensor", out=out[:, sl], in0=src[:, sl],
                 in1=ps[abk][:, :], op=AL.mult,
                 inc=(psf[abk], 1) if db == 7 else None)
        p.wait("tensor", psf[abk])
        p.op("tensor", "matmul", ps[abk][:, :], ones_row[:, :],
             lnrv[0:1, 0:512], start=True, stop=True, inc=(psm[abk], 1))
        ln_state["lnr_guard"] = spe.n + 0
        p.op("tensor", "nop", inc=(spe, 1))
        ln_state["lnr_guard"] = spe.n
        p.wait("vector", psm[abk])
        for db in range(8):
            sl = slice(db * 512, (db + 1) * 512)
            p.op("vector", "tensor_add", out=out[:, sl], in0=out[:, sl],
                 in1=ps[abk][:, :],
                 inc=(psf[abk], 1) if db == 7 else None)
        if gam is not None:
            for db in range(8):
                sl = slice(db * 512, (db + 1) * 512)
                p.op("vector", "tensor_scalar", out=out[:, sl], in0=out[:, sl],
                     scalar1=gam[:, db:db + 1], scalar2=bet[:, db:db + 1],
                     op0=AL.mult, op1=AL.add,
                     inc=(sve, 1) if db == 7 else None)
        else:
            p.op("vector", "drain", fusable=False, inc=(sve, 1))
        return sve.n

    tok_dma2 = {}

    def exit_tile(src, nt, tokoutv):
        for tt in range(4):
            tb = tt % 2
            tko = tokoutv[:, tb * 1024:(tb + 1) * 1024]
            for db in range(8):
                bk = 4 + (db // 4)
                if db % 4 == 0:
                    p.wait("tensor", psf[bk])
                p.op("tensor", "transpose",
                     ps[bk][:, (db % 4) * 128:(db % 4 + 1) * 128],
                     src[:, db * 512 + tt * 128: db * 512 + (tt + 1) * 128],
                     ident_t[:, :], inc=(psm[bk], 1) if db % 4 == 3 else None)
            if (nt, tt - 2) in tok_dma2:
                p.wait("vector", sld, tok_dma2[(nt, tt - 2)])
            elif tt < 2 and (nt - 1, 2 + tt) in tok_dma2:
                p.wait("vector", sld, tok_dma2[(nt - 1, 2 + tt)])
            p.wait("vector", psm[4])
            p.op("vector", "tensor_copy", out=tko[:, 0:512], in_=ps[4][:, :],
                 inc=(psf[4], 1))
            p.wait("vector", psm[5])
            p.op("vector", "tensor_copy", out=tko[:, 512:1024], in_=ps[5][:, :],
                 inc=[(psf[5], 1), (sve, 1)])
            p.wait("sync", sve)
            p.dma("sync",
                  out_final[nt * 512 + tt * 128: nt * 512 + (tt + 1) * 128, :],
                  tko[:, :], sld)
            tok_dma2[(nt, tt)] = sld.n

    if phases != "A":
        p.barrier(bsem)
        AR.reset(0)
        w1f8 = AR.alloc8(8 * 4096)
        w2f8 = AR.alloc8(32 * 1024)
        b1q = AR.alloc(32)
        b2q = AR.alloc(8)
        bq8 = AR.alloc8(32)
        s1t = AR.alloc(2 * 8 * 512)
        stageB = s1t[:, 0:4096]
        s2t = AR.alloc(2 * 8 * 512)
        lnrB = AR.alloc(2048)
        xq8 = AR.alloc8(8 * 512)
        h8 = AR.alloc8(32 * 512)
        hbst = AR.alloc16(2 * 512)
        maskt = AR.alloc16(2 * 8 * 512)
        sbfB = AR.alloc16(8 * 512)
        sqB = AR.alloc16(8 * 512)
        ln_state["lnr_guard"] = 0

        load_weight_cast(w1f8, w1t_i, 8, 4096, stageB)
        load_weight_cast(w2f8, w2t_i, 32, 1024, stageB)
        p.wait("sync", ssc)
        p.dma("sync", stageB[:, 0:32], b1c_i[:, :], sld)
        p.dma("sync", stageB[:, 32:40], b2c_i[:, :], sld)
        p.wait("scalar", sld)
        p.op("scalar", "copy", out=bq8[:, 0:32], in_=stageB[:, 0:32])
        p.op("scalar", "copy", out=b1q[:, :], in_=bq8[:, 0:32])
        p.op("scalar", "copy", out=bq8[:, 0:8], in_=stageB[:, 32:40])
        p.op("scalar", "copy", out=b2q[:, :], in_=bq8[:, 0:8], inc=(ssc, 1))
        bq_rdy = ssc.n

        bdone = {}
        for nt in range(NTH):
            par = nt % 2
            tsl = slice(nt * 512, (nt + 1) * 512)
            s1v = s1t[:, par * 4096:(par + 1) * 4096]
            s2v = s2t[:, par * 4096:(par + 1) * 4096]
            mkv = maskt[:, par * 4096:(par + 1) * 4096]
            if nt == 0:
                p.wait("sync", ssc, bq_rdy)
            if nt >= 2:
                p.wait("sync", sve, bdone[nt - 2][0])
                p.wait("sync", spe, bdone[nt - 2][1])
            p.dma("sync", s1v[:, :].rearrange("p (a t) -> p a t", a=8),
                  s1_hbm[:, tsl].rearrange("(a p) t -> p a t", p=128), sld)
            p.dma("sync", mkv[:, :].rearrange("p (a t) -> p a t", a=8),
                  mask_i[:, tsl].rearrange("(a p) t -> p a t", p=128), sld)
            s1_ld = sld.n
            p.wait("vector", sld, s1_ld)
            if nt >= 1:
                p.wait("vector", spe, bdone[nt - 1][1])
            for db in range(8):
                p.op("vector", "tensor_copy",
                     out=xq8[:, db * 512:(db + 1) * 512],
                     in_=s1v[:, db * 512:(db + 1) * 512],
                     inc=(sve, 1) if db == 7 else None)
            xq_rdy = sve.n
            p.wait("tensor", sve, xq_rdy)
            p.wait("tensor", ssc, bq_rdy)
            for mb in range(32):
                bank = mb % 3
                p.wait("tensor", psf[bank])
                for kb in range(8):
                    p.op("tensor", "matmul", ps[bank][:, :],
                         w1f8[:, kb * 4096 + mb * 128: kb * 4096 + (mb + 1) * 128],
                         xq8[:, kb * 512:(kb + 1) * 512],
                         start=(kb == 0), stop=(kb == 7),
                         inc=(psm[bank], 1) if kb == 7 else None)
                hb = hbst[:, (mb % 2) * 512:(mb % 2 + 1) * 512]
                if mb >= 2:
                    p.wait("scalar", sve)
                p.wait("scalar", psm[bank])
                p.op("scalar", "activation", out=hb[:, :], in_=ps[bank][:, :],
                     func=AF.Relu, bias=b1q[:, mb:mb + 1],
                     inc=[(psf[bank], 1), (ssc, 1)])
                p.wait("vector", ssc)
                p.op("vector", "tensor_copy",
                     out=h8[:, mb * 512:(mb + 1) * 512], in_=hb[:, :],
                     inc=(sve, 1))
            h8_rdy = sve.n
            p.wait("tensor", sve, h8_rdy)
            for mb in range(8):
                bank = 3 + (mb % 2)
                p.wait("tensor", psf[bank])
                for kb in range(32):
                    p.op("tensor", "matmul", ps[bank][:, :],
                         w2f8[:, kb * 1024 + mb * 128: kb * 1024 + (mb + 1) * 128],
                         h8[:, kb * 512:(kb + 1) * 512],
                         start=(kb == 0), stop=(kb == 31),
                         inc=(psm[bank], 1) if kb == 31 else None)
                hb = hbst[:, (mb % 2) * 512:(mb % 2 + 1) * 512]
                if True:
                    p.wait("scalar", sve)
                p.wait("scalar", psm[bank])
                p.op("scalar", "activation", out=hb[:, :], in_=ps[bank][:, :],
                     func=AF.Identity, bias=b2q[:, mb:mb + 1],
                     inc=[(psf[bank], 1), (ssc, 1)])
                sl = slice(mb * 512, (mb + 1) * 512)
                p.wait("vector", ssc)
                p.op("vector", "tensor_tensor", out=hb[:, :], in0=hb[:, :],
                     in1=mkv[:, sl], op=AL.mult)
                p.op("vector", "tensor_add", out=s2v[:, sl], in0=s1v[:, sl],
                     in1=hb[:, :], inc=(sve, 1))
            s2_rdy = sve.n
            p.wait("vector", sve, s2_rdy)
            ln_tile(s2v, s2v, lnrB[0:1, :], sbfB, sqB, (5, 6), 7)
            p.op("tensor", "nop", inc=(spe, 1))
            bdone[nt] = (sve.n, spe.n)
            p.wait("sync", sve, bdone[nt][0])
            p.dma("sync", s2_hbm[:, tsl].rearrange("(a p) t -> p a t", p=128),
                  s2v[:, :].rearrange("p (a t) -> p a t", a=8), sld)
            bdone[nt] = (sve.n, spe.n, sld.n)
        if "s2" in dev_out:
            p.wait("sync", sld)
            p.dma("sync", dev_out["s2"][:, :], s2_hbm[:, :], sld)

        p.barrier(bsem)
        AR.reset(0)
        aw1b = AR.alloc16(8 * 1024)
        wwtb = AR.alloc16(8 * 496)
        aw2b = AR.alloc16(8 * 1024)
        ab1v = AR.alloc(8)
        ab2v = AR.alloc(8)
        gamv = AR.alloc(8)
        betv = AR.alloc(8)
        bwb = AR.alloc16(512)
        onesb = AR.alloc16(128)
        s2rt = AR.alloc(2 * 8 * 512)
        stageC = s2rt[:, 0:1024]
        hbf = AR.alloc16(2 * 8 * 512)

        load_weight_cast(aw1b, aw1t_i, 8, 1024, stageC)
        load_weight_cast(wwtb, wwt_i, 8, 496, stageC)
        load_weight_cast(aw2b, aw2t_i, 8, 1024, stageC)
        p.wait("sync", ssc)
        p.dma("sync", ab1v[:, :], ab1_i[:, :], sld)
        p.dma("sync", ab2v[:, :], ab2_i[:, :], sld)
        p.dma("sync", gamv[:, :], gam_i[:, :], sld)
        p.dma("sync", betv[:, :], bet_i[:, :], sld)
        p.dma("sync", stageC[0:1, 0:496], bw_i[:, :], sld)
        p.wait("scalar", sld)
        p.op("scalar", "copy", out=bwb[0:1, 0:496], in_=stageC[0:1, 0:496])
        p.wait("vector", sld)
        p.op("vector", "memset", onesb[0:1, :], 1.0)
        p.op("vector", "memset", bwb[0:1, 496:512], 0.0, inc=(sve, 1))
        p.op("scalar", "drain", fusable=False, inc=(ssc, 1))
        c1_const = (sve.n, ssc.n)

        xbfC = AR.alloc16(2 * 8 * 512)

        c1done = {}
        for nt in range(NTH):
            par = nt % 2
            tsl = slice(nt * 512, (nt + 1) * 512)
            s2v = s2rt[:, par * 4096:(par + 1) * 4096]
            hbv = hbf[:, par * 4096:(par + 1) * 4096]
            xbv = xbfC[:, par * 4096:(par + 1) * 4096]
            if nt == 0:
                p.wait("sync", ssc, c1_const[1])
                p.wait("sync", sve, c1_const[0])
            if nt >= 2:
                p.wait("sync", sve, c1done[nt - 2][0])
                p.wait("sync", sld, c1done[nt - 2][2])
            p.dma("sync", s2v[:, :].rearrange("p (a t) -> p a t", a=8),
                  s2_hbm[:, tsl].rearrange("(a p) t -> p a t", p=128), sld)
            sld_t = sld.n
            p.wait("vector", sld, sld_t)
            p.wait("vector", sve, c1_const[0])
            if nt >= 2:
                p.wait("vector", spe, c1done[nt - 2][1])
            for db in range(8):
                p.op("vector", "tensor_copy",
                     out=xbv[:, db * 512:(db + 1) * 512],
                     in_=s2v[:, db * 512:(db + 1) * 512],
                     inc=(sve, 1) if db == 7 else None)
            x_rdy = sve.n
            p.wait("tensor", sve, x_rdy)
            p.wait("tensor", ssc, c1_const[1])
            for mb in range(8):
                bank = mb % 2
                p.wait("tensor", psf[bank])
                for kb in range(8):
                    p.op("tensor", "matmul", ps[bank][:, :],
                         aw1b[:, kb * 1024 + mb * 128: kb * 1024 + (mb + 1) * 128],
                         xbv[:, kb * 512:(kb + 1) * 512],
                         start=(kb == 0), stop=(kb == 7),
                         inc=(psm[bank], 1) if kb == 7 else None)
                p.wait("scalar", psm[bank])
                p.op("scalar", "activation", out=hbv[:, mb * 512:(mb + 1) * 512],
                     in_=ps[bank][:, :], func=AF.Identity,
                     bias=ab1v[:, mb:mb + 1],
                     inc=[(psf[bank], 1)] + ([(ssc, 1)] if mb == 7 else []))
            h_rdy = ssc.n
            p.op("tensor", "nop", inc=(spe, 1))
            p.wait("sync", ssc, h_rdy)
            p.dma("sync", hag_in[:, tsl].rearrange("(a p) t -> p a t", p=128),
                  hbv[:, :].rearrange("p (a t) -> p a t", a=8), sld)
            if "hC" in dev_out:
                p.dma("sync", dev_out["hC"][:, tsl].rearrange("(a p) t -> p a t", p=128),
                      hbv[:, :].rearrange("p (a t) -> p a t", a=8), sld)
            c1done[nt] = (sve.n, spe.n, sld.n)

        p.wait("gpsimd", sld, c1done[NTH - 1][2])
        p.op("gpsimd", "collective_compute", "AllGather", AL.bypass,
             replica_groups=[[0, 2, 4, 6], [1, 3, 5, 7]],
             ins=[hag_in[:, :].opt()], outs=[hag_out[:, :, :].opt()],
             inc=(scc, 1))
        hag_done = scc.n

        sel_sb = AR.alloc16(4 * 64)
        wsmt = AR.alloc(512)
        wsegs = AR.alloc(32)
        wsmTb = AR.alloc16(4 * 128)
        wtapb = AR.alloc16(512)
        cacc = AR.alloc(8 * 512)
        caccb = AR.alloc16(8 * 512)
        ctmp = AR.alloc(512)
        hjt = AR.alloc16(4 * 512)
        sit = AR.alloc(8 * 512)
        s3t = AR.alloc(8 * 512)
        lnrC = AR.alloc(2048)
        sbfC = AR.alloc16(8 * 512)
        sqC = AR.alloc16(8 * 512)
        rsel_sb = AR.alloc16(32 * 128)

        p.wait("sync", scc, hag_done)
        for bb in range(4):
            p.dma("sync", sel_sb[:, bb * 64:(bb + 1) * 64], sel_i[bb, :, :], sld)
        for cb in range(8):
            for j in range(4):
                u = cb * 4 + j
                p.dma("sync", rsel_sb[0:64, u * 128:(u + 1) * 128],
                      rsel_i[cb, j, :, :], sld)
        sel_ld = sld.n

        p.op("scalar", "memzero", wsmTb[:, 384:512])
        p.op("scalar", "drain", fusable=False, inc=(ssc, 1))

        c2done = {}
        hj_dma = {}
        for nt in range(NTH):
            tsl = slice(nt * 512, (nt + 1) * 512)
            par = nt % 2
            s2v = s2rt[:, par * 4096:(par + 1) * 4096]
            hbv = hbf[:, par * 4096:(par + 1) * 4096]
            if nt >= 2:
                p.wait("sync", sve, c2done[nt - 2][0])
                p.wait("sync", spe, c2done[nt - 2][1])
            p.dma("sync", s2v[:, :].rearrange("p (a t) -> p a t", a=8),
                  s2_hbm[:, tsl].rearrange("(a p) t -> p a t", p=128), sld)
            p.dma("sync", hbv[:, :].rearrange("p (a t) -> p a t", a=8),
                  hag_in[:, tsl].rearrange("(a p) t -> p a t", p=128), sld)
            c2_ld = sld.n

            p.wait("tensor", sld, c2_ld)
            p.wait("tensor", sld, sel_ld)
            if nt >= 1:
                p.wait("tensor", spe, c2done[nt - 1][1])
            widths = (128, 128, 128, 112)
            for tc in range(4):
                lb = 2 + (tc % 2)
                p.wait("tensor", psf[lb])
                for kb in range(8):
                    p.op("tensor", "matmul", ps[lb][:, 0:496],
                         hbv[:, kb * 512 + tc * 128: kb * 512 + (tc + 1) * 128],
                         wwtb[:, kb * 496:(kb + 1) * 496],
                         start=(kb == 0), stop=False)
                p.op("tensor", "matmul", ps[lb][:, 0:496], onesb[0:1, 0:128],
                     bwb[0:1, 0:496], start=False, stop=True, inc=(psm[lb], 1))
                p.wait("vector", psm[lb])
                p.op("vector", "reduce_max", out=wsegs[:, 16:17],
                     in_=ps[lb][:, 0:496], axis=AX.X)
                p.op("vector", "tensor_scalar_mul", out=wsegs[:, 16:17],
                     in0=wsegs[:, 16:17], scalar1=-1.0, inc=(sve, 1))
                p.wait("scalar", sve)
                p.op("scalar", "activation", out=wsmt[:, 0:496],
                     in_=ps[lb][:, 0:496], func=AF.Exp,
                     bias=wsegs[:, 16:17], inc=(ssc, 1))
                p.wait("vector", ssc)
                p.op("vector", "tensor_reduce", out=wsegs[:, 0:16],
                     in_=wsmt[:, 0:496].rearrange("p (h k) -> p h k", k=31),
                     axis=AX.X, op=AL.add, inc=(psf[lb], 1))
                p.op("vector", "drain", fusable=False)
                p.op("vector", "reciprocal", out=wsegs[:, 0:16],
                     in_=wsegs[:, 0:16])
                for hd in range(16):
                    p.op("vector", "tensor_scalar",
                         out=wsmt[:, hd * 31:(hd + 1) * 31],
                         in0=wsmt[:, hd * 31:(hd + 1) * 31],
                         scalar1=wsegs[:, hd:hd + 1], scalar2=None,
                         op0=AL.mult, inc=(sve, 1) if hd == 15 else None)
                sm_rdy = sve.n
                p.wait("tensor", sve, sm_rdy)
                for bb in range(4):
                    tb = bb % 2
                    p.wait("tensor", psf[tb])
                    p.op("tensor", "transpose", ps[tb][0:widths[bb], 0:128],
                         wsmt[:, bb * 128: bb * 128 + widths[bb]],
                         ident_t[:, :], inc=(psm[tb], 1))
                    p.wait("scalar", psm[tb])
                    p.op("scalar", "copy",
                         out=wsmTb[0:widths[bb], bb * 128:(bb + 1) * 128],
                         in_=ps[tb][0:widths[bb], 0:128],
                         inc=[(psf[tb], 1)] + ([(ssc, 1)] if bb == 3 else []))
                wsmT_rdy = ssc.n
                p.wait("tensor", ssc, wsmT_rdy)
                p.wait("tensor", psf[4])
                for bb in range(4):
                    p.op("tensor", "matmul", ps[4][0:64, 0:128],
                         sel_sb[:, bb * 64:(bb + 1) * 64],
                         wsmTb[:, bb * 128:(bb + 1) * 128],
                         start=(bb == 0), stop=(bb == 3),
                         inc=(psm[4], 1) if bb == 3 else None)
                p.wait("scalar", psm[4])
                p.op("scalar", "copy", out=wtapb[0:64, tc * 128:(tc + 1) * 128],
                     in_=ps[4][0:64, 0:128], inc=[(psf[4], 1), (ssc, 1)])
                if "wsm" in dev_out:
                    p.wait("sync", sve)
                    p.dma("sync",
                          dev_out["wsm"][nt * 512 + tc * 128:
                                         nt * 512 + (tc + 1) * 128, 0:496],
                          wsmt[:, 0:496], sld)
            wtap_rdy = ssc.n

            p.wait("tensor", ssc, wtap_rdy)
            for cb in range(8):
                for j in range(4):
                    u = cb * 4 + j
                    slot = u % 4
                    prev = hj_dma.get((nt, u - 4)) or (
                        hj_dma.get((nt - 1, 28 + u)) if nt >= 1 else None)
                    if prev:
                        p.wait("sync", sve, prev[1])
                    p.wait("sync", scc, hag_done)
                    p.dma("sync", hjt[:, slot * 512:(slot + 1) * 512],
                          hag_out[j, cb * 128:(cb + 1) * 128, tsl], sld)
                    hj_dma[(nt, u)] = (sld.n, None)
                    cbank = 5 + (u % 2)
                    p.wait("tensor", psf[cbank])
                    p.op("tensor", "matmul", ps[cbank][:, :],
                         rsel_sb[0:64, u * 128:(u + 1) * 128],
                         wtapb[0:64, 0:512],
                         start=True, stop=True, inc=(psm[cbank], 1))
                    p.wait("vector", psm[cbank])
                    p.wait("vector", sld, hj_dma[(nt, u)][0])
                    if j == 0:
                        p.op("vector", "tensor_tensor",
                             out=cacc[:, cb * 512:(cb + 1) * 512],
                             in0=hjt[:, slot * 512:(slot + 1) * 512],
                             in1=ps[cbank][:, :], op=AL.mult,
                             inc=[(psf[cbank], 1), (sve, 1)])
                    else:
                        p.op("vector", "tensor_tensor", out=ctmp[:, :],
                             in0=hjt[:, slot * 512:(slot + 1) * 512],
                             in1=ps[cbank][:, :], op=AL.mult,
                             inc=(psf[cbank], 1))
                        p.op("vector", "tensor_add",
                             out=cacc[:, cb * 512:(cb + 1) * 512],
                             in0=cacc[:, cb * 512:(cb + 1) * 512],
                             in1=ctmp[:, :], inc=(sve, 1))
                    hj_dma[(nt, u)] = (hj_dma[(nt, u)][0], sve.n)
                p.op("vector", "tensor_copy",
                     out=caccb[:, cb * 512:(cb + 1) * 512],
                     in_=cacc[:, cb * 512:(cb + 1) * 512],
                     inc=(sve, 1) if cb == 7 else None)
            conv_rdy = sve.n
            if "conv" in dev_out:
                p.wait("sync", sve, conv_rdy)
                p.dma("sync",
                      dev_out["conv"][:, tsl].rearrange("(a p) t -> p a t", p=128),
                      cacc[:, :].rearrange("p (a t) -> p a t", a=8), sld)

            p.wait("tensor", sve, conv_rdy)
            for mb in range(8):
                bank = mb % 2
                p.wait("tensor", psf[bank])
                for kb in range(8):
                    p.op("tensor", "matmul", ps[bank][:, :],
                         aw2b[:, kb * 1024 + mb * 128: kb * 1024 + (mb + 1) * 128],
                         caccb[:, kb * 512:(kb + 1) * 512],
                         start=(kb == 0), stop=(kb == 7),
                         inc=(psm[bank], 1) if kb == 7 else None)
                sl = slice(mb * 512, (mb + 1) * 512)
                p.wait("scalar", psm[bank])
                p.op("scalar", "activation", out=sit[:, sl],
                     in_=ps[bank][:, :], func=AF.Identity,
                     bias=ab2v[:, mb:mb + 1], inc=[(psf[bank], 1), (ssc, 1)])
                p.wait("vector", ssc)
                p.op("vector", "tensor_add", out=sit[:, sl], in0=sit[:, sl],
                     in1=s2v[:, sl], inc=(sve, 1))
            si_rdy = sve.n
            p.wait("vector", sve, si_rdy)
            ln_tile(sit, sit, lnrC[0:1, :], sbfC, sqC, (2, 3), 4,
                    gam=gamv, bet=betv)
            for db in range(8):
                sl = slice(db * 512, (db + 1) * 512)
                p.op("vector", "tensor_add", out=s3t[:, sl], in0=s2v[:, sl],
                     in1=sit[:, sl], inc=(sve, 1) if db == 7 else None)
            ln_tile(s3t, s3t, lnrC[0:1, :], sbfC, sqC, (2, 3), 4)
            p.op("tensor", "nop", inc=(spe, 1))
            s3_rdy = sve.n
            p.wait("sync", sve, s3_rdy)
            p.dma("sync", s3_hbm[:, tsl].rearrange("(a p) t -> p a t", p=128),
                  s3t[:, :].rearrange("p (a t) -> p a t", a=8), sld)
            c2done[nt] = (sve.n, spe.n, sld.n)
        if "s3" in dev_out:
            p.wait("sync", sld)
            p.dma("sync", dev_out["s3"][:, :], s3_hbm[:, :], sld)

        p.barrier(bsem)
        AR.reset(0)
        repd_sb = AR.alloc16(128)
        repg_sb = AR.alloc16(128)
        acb = AR.alloc16(32)
        bcb = AR.alloc16(16)
        arep = AR.alloc(32)
        brep = AR.alloc(16)
        aexp = AR.alloc(512)
        bexp = AR.alloc(512)
        wtmp = AR.alloc(512)
        wacc = AR.alloc(512)
        wrtb = AR.alloc16(4 * 512)
        witb = AR.alloc16(4 * 512)
        winegb = AR.alloc16(4 * 512)
        brcv = AR.alloc(4)
        bicv = AR.alloc(4)
        stageD = AR.alloc(64)
        s3rt = AR.alloc(2 * 8 * 512)
        x3b = AR.alloc16(8 * 512)
        y4t = AR.alloc(8 * 512)
        lnrD = AR.alloc(2048)
        sbfD = AR.alloc16(8 * 512)
        sqD = AR.alloc16(8 * 512)
        tokoutD = AR.alloc(2 * 1024)

        p.wait("sync", sld)
        p.dma("sync", repd_sb[0:8, :], repd_i[:, :], sld)
        p.dma("sync", repg_sb[0:16, :], repg_i[:, :], sld)
        p.dma("sync", brcv[:, :], brc_i[:, :], sld)
        p.dma("sync", bicv[:, :], bic_i[:, :], sld)
        rep_ld = sld.n

        for wi, (g1d, g2d, outb) in enumerate(
                ((g1r_i, g2r_i, wrtb), (g1i_i, g2i_i, witb))):
            for kb in range(4):
                for c in range(4):
                    p.wait("sync", ssc)
                    p.wait("sync", sve)
                    p.dma("sync", stageD[0:8, 0:32], g1d[c, kb, :, :], sld)
                    p.dma("sync", stageD[0:16, 32:48], g2d[c, :, :], sld)
                    p.wait("scalar", sld)
                    p.wait("scalar", psm[0])
                    p.wait("scalar", psm[1])
                    p.op("scalar", "copy", out=acb[0:8, :],
                         in_=stageD[0:8, 0:32])
                    p.op("scalar", "copy", out=bcb[0:16, :],
                         in_=stageD[0:16, 32:48], inc=(ssc, 1))
                    p.wait("tensor", ssc)
                    p.wait("tensor", sld, rep_ld)
                    p.wait("tensor", psf[0])
                    p.op("tensor", "matmul", ps[0][:, 0:32], repd_sb[0:8, :],
                         acb[0:8, :], start=True, stop=True, inc=(psm[0], 1))
                    p.wait("tensor", psf[1])
                    p.op("tensor", "matmul", ps[1][:, 0:16], repg_sb[0:16, :],
                         bcb[0:16, :], start=True, stop=True, inc=(psm[1], 1))
                    p.wait("vector", psm[0])
                    p.wait("vector", psm[1])
                    p.op("vector", "tensor_copy", out=arep[:, :],
                         in_=ps[0][:, 0:32], inc=(psf[0], 1))
                    p.op("vector", "tensor_copy", out=brep[:, :],
                         in_=ps[1][:, 0:16], inc=(psf[1], 1))
                    p.op("vector", "tensor_copy",
                         out=aexp[:, :].rearrange("p (b e) -> p b e", e=16),
                         in_=arep[:, :].unsqueeze(2).broadcast_to([128, 32, 16]))
                    p.op("vector", "tensor_copy",
                         out=bexp[:, :].rearrange("p (b e) -> p b e", e=16),
                         in_=brep[:, :].unsqueeze(1).broadcast_to([128, 32, 16]))
                    if c == 0:
                        p.op("vector", "tensor_tensor", out=wacc[:, :],
                             in0=aexp[:, :], in1=bexp[:, :], op=AL.mult,
                             inc=(sve, 1))
                    else:
                        p.op("vector", "tensor_tensor", out=wtmp[:, :],
                             in0=aexp[:, :], in1=bexp[:, :], op=AL.mult)
                        p.op("vector", "tensor_add", out=wacc[:, :],
                             in0=wacc[:, :], in1=wtmp[:, :], inc=(sve, 1))
                p.op("vector", "tensor_copy",
                     out=outb[:, kb * 512:(kb + 1) * 512], in_=wacc[:, :],
                     inc=(sve, 1))
        p.op("vector", "tensor_scalar_mul", out=winegb[:, :], in0=witb[:, :],
             scalar1=-1.0, inc=(sve, 1))
        wbuild_done = sve.n

        ddone = {}
        for nt in range(NTH):
            tsl = slice(nt * 512, (nt + 1) * 512)
            par = nt % 2
            s3v = s3rt[:, par * 4096:(par + 1) * 4096]
            if nt >= 2:
                p.wait("sync", sve, ddone[nt - 2][0])
            p.dma("sync", s3v[:, :].rearrange("p (a t) -> p a t", a=8),
                  s3_hbm[:, tsl].rearrange("(a p) t -> p a t", p=128), sld)
            s3_ld = sld.n
            p.wait("vector", sld, s3_ld)
            p.wait("vector", sve, wbuild_done)
            if nt >= 1:
                p.wait("vector", spe, ddone[nt - 1][1])
            for db in range(8):
                p.op("vector", "tensor_copy",
                     out=x3b[:, db * 512:(db + 1) * 512],
                     in_=s3v[:, db * 512:(db + 1) * 512],
                     inc=(sve, 1) if db == 7 else None)
            x3_rdy = sve.n
            p.wait("tensor", sve, x3_rdy)
            for part in range(2):
                for mb in range(4):
                    blk = part * 4 + mb
                    bank = blk % 4
                    p.wait("tensor", psf[bank])
                    wa = wrtb if part == 0 else witb
                    wb2 = winegb if part == 0 else wrtb
                    for kb in range(4):
                        p.op("tensor", "matmul", ps[bank][:, :],
                             wa[:, kb * 512 + mb * 128: kb * 512 + (mb + 1) * 128],
                             x3b[:, kb * 512:(kb + 1) * 512],
                             start=(kb == 0), stop=False)
                    for kb in range(4):
                        p.op("tensor", "matmul", ps[bank][:, :],
                             wb2[:, kb * 512 + mb * 128: kb * 512 + (mb + 1) * 128],
                             x3b[:, (4 + kb) * 512:(5 + kb) * 512],
                             start=False, stop=(kb == 3),
                             inc=(psm[bank], 1) if kb == 3 else None)
                    bv = brcv if part == 0 else bicv
                    sl = slice(blk * 512, (blk + 1) * 512)
                    p.wait("scalar", psm[bank])
                    p.op("scalar", "activation", out=y4t[:, sl],
                         in_=ps[bank][:, :], func=AF.Identity,
                         bias=bv[:, mb:mb + 1], inc=[(psf[bank], 1), (ssc, 1)])
                    p.wait("vector", ssc)
                    p.op("vector", "tensor_add", out=y4t[:, sl],
                         in0=y4t[:, sl], in1=s3v[:, sl], inc=(sve, 1))
            y4_rdy = sve.n
            p.wait("vector", sve, y4_rdy)
            ln_tile(y4t, y4t, lnrD[0:1, :], sbfD, sqD, (2, 3), 6)
            p.op("tensor", "nop", inc=(spe, 1))
            exit_tile(y4t, nt, tokoutD)
            ddone[nt] = (sve.n, spe.n)

    p.wait("sync", sld)
    p.wait("gpsimd", sg)
    p.barrier(bsem)

    p.finalize()
    stack.close()
    return nc



def _prep_inputs(inputs):
    f = np.ascontiguousarray
    x = np.asarray(inputs["L0_attn_x"])
    ipw_full = np.asarray(inputs["L0_attn_in_proj_w"])
    convw = np.asarray(inputs["L0_attn_conv_w"])
    convb = np.asarray(inputs["L0_attn_conv_b"])
    xpw = np.asarray(inputs["L0_attn_x_proj_w"])
    dtw = np.asarray(inputs["L0_attn_dt_proj_w"])
    dtb = np.asarray(inputs["L0_attn_dt_proj_b"])
    alog = np.asarray(inputs["L0_attn_A_log"])
    dparam = np.asarray(inputs["L0_attn_D_param"])
    cpw = np.asarray(inputs["L0_attn_c_proj_w"])
    ident = np.eye(128, dtype=np.float32)

    W1 = np.asarray(inputs["L0_ffn_W1"])
    b1 = np.asarray(inputs["L0_ffn_b1"])
    W2 = np.asarray(inputs["L0_ffn_W2"])
    b2 = np.asarray(inputs["L0_ffn_b2"])
    w1t = f(W1.T)
    w2t = f(W2.T)
    b1c = f(b1.reshape(32, 128).T)
    b2c = f(b2.reshape(8, 128).T)
    import jax
    with jax.default_device(jax.devices("cpu")[0]):
        keep = np.asarray(jax.random.bernoulli(
            jax.random.key(42, impl="threefry2x32"), 0.8, (B, T, D)))
    maskq = np.where(keep, np.float32(1.25), np.float32(0.0)).astype(BFNP)

    aw1 = np.asarray(inputs["L1_attn_w1"])
    ab1f = np.asarray(inputs["L1_attn_b1"])
    ww = np.asarray(inputs["L1_attn_ww"])
    bwf = np.asarray(inputs["L1_attn_bw"])
    aw2 = np.asarray(inputs["L1_attn_w2"])
    ab2f = np.asarray(inputs["L1_attn_b2"])
    gamf = np.asarray(inputs["L1_attn_gamma"])
    betf = np.asarray(inputs["L1_attn_beta"])
    aw1t = f(aw1.T)
    wwt = f(ww.T)
    aw2t = f(aw2.T)
    ab1 = f(ab1f.reshape(8, 128).T)
    ab2 = f(ab2f.reshape(8, 128).T)
    gamr = f(gamf.reshape(8, 128).T)
    betr = f(betf.reshape(8, 128).T)
    bwv = f(bwf[None, :])
    sel_by_core = []
    for bb_ in range(4):
        selm = np.zeros((512, 64), np.float32)
        for hd in range(16):
            for j in range(4):
                if j <= bb_:
                    col = hd * 31 + (30 - bb_ + j)
                    selm[col, hd * 4 + j] = 1.0
        sel_by_core.append(np.ascontiguousarray(
            selm.reshape(4, 128, 64).astype(BFNP)))
    rsel = np.zeros((8, 4, 64, 128), np.float32)
    for cb in range(8):
        for j in range(4):
            for hl in range(2):
                hd = 2 * cb + hl
                rsel[cb, j, hd * 4 + j, hl * 64:(hl + 1) * 64] = 1.0
    rsel = np.ascontiguousarray(rsel.astype(BFNP))

    G1R = np.asarray(inputs["L1_ffn_G1_R"])
    G2R = np.asarray(inputs["L1_ffn_G2_R"])
    G1I = np.asarray(inputs["L1_ffn_G1_I"])
    G2I = np.asarray(inputs["L1_ffn_G2_I"])
    def g1_lay(G):
        a = G[0].transpose(1, 2, 0)
        return f(a.reshape(4, 4, 8, 32))
    def g2_lay(G):
        return f(G[:, :, 0, :].transpose(0, 2, 1))
    g1r, g1i = g1_lay(G1R), g1_lay(G1I)
    g2r, g2i = g2_lay(G2R), g2_lay(G2I)
    repd = np.zeros((8, 128), np.float32)
    for dl in range(8):
        repd[dl, dl * 16:(dl + 1) * 16] = 1.0
    repd = np.ascontiguousarray(repd.astype(BFNP))
    repg = np.zeros((16, 128), np.float32)
    for g_ in range(16):
        repg[g_, g_::16] = 1.0
    repg = np.ascontiguousarray(repg.astype(BFNP))
    brf = np.asarray(inputs["L1_ffn_bias_real"])
    bif = np.asarray(inputs["L1_ffn_bias_imag"])
    brc = f(brf.reshape(4, 128).T)
    bic = f(bif.reshape(4, 128).T)

    in_maps = []
    for core in range(8):
        b, h = core // 2, core % 2
        hs = slice(h * DIH, (h + 1) * DIH)
        m = {
            "x_tok": f(x[b]),
            "x_tok_half": f(x[b, h * (T // 2):(h + 1) * (T // 2)]),
            "ipw": f(np.concatenate(
                [ipw_full[:, h * DIH:(h + 1) * DIH],
                 ipw_full[:, DI + h * DIH: DI + (h + 1) * DIH]], axis=1)),
            "convw": f(convw[hs, 0, :]),
            "convb": f(convb[hs, None]),
            "xpw": f(xpw[hs, :]),
            "dtw": f(dtw[0, hs, None]),
            "dtb": f(dtb[hs, None]),
            "alog": f(alog[hs, :]),
            "dparam": f(dparam[hs, None]),
            "cpwh": f(cpw[hs, :]),
            "ident": ident,
            "w1t": w1t,
            "b1c": b1c,
            "w2t": w2t,
            "b2c": b2c,
            "mask": f(maskq[b, h * (T // 2):(h + 1) * (T // 2), :].T),
            "aw1t": aw1t,
            "ab1": ab1,
            "wwt": wwt,
            "bw": bwv,
            "aw2t": aw2t,
            "ab2": ab2,
            "gam": gamr,
            "bet": betr,
            "sel": sel_by_core[b],
            "rsel": rsel,
            "g1r": g1r, "g2r": g2r, "g1i": g1i, "g2i": g2i,
            "repd": repd, "repg": repg,
            "brc": brc, "bic": bic,
        }
        in_maps.append(m)
    return in_maps


_CACHE = {}


def kernel(**inputs):
    key = "full"
    if key not in _CACHE:
        _CACHE[key] = build_program(phases="ABCD")
    nc = _CACHE[key]
    in_maps = _prep_inputs(inputs)
    res = run_bass_kernel_spmd(nc, in_maps, core_ids=list(range(8)))
    out = np.empty((B, T, D), dtype=np.float32)
    for core in range(8):
        b, h = core // 2, core % 2
        out[b, h * (T // 2):(h + 1) * (T // 2)] = res.results[core]["out_tok"]
    return out


# version 05
# speedup vs baseline: 1.2218x; 1.2218x over previous
import numpy as np
import ml_dtypes
import concourse.bass as bass
import concourse.mybir as mybir
from concourse.bass_utils import run_bass_kernel_spmd

F32 = mybir.dt.float32
BF16 = mybir.dt.bfloat16
FP8 = mybir.dt.float8e5
AL = mybir.AluOpType
AF = mybir.ActivationFunctionType
AX = mybir.AxisListType
BFNP = ml_dtypes.bfloat16

B, D, DI, S = 4, 1024, 2048, 16
T = 8192
DIH = DI // 2

ENGINES = ("sync", "tensor", "vector", "scalar", "gpsimd")
NO_HALO = False
RAW_SILU = False
NO_CONV = False


class Sem:
    def __init__(self, handle):
        self.h = handle
        self.n = 0


class Prog:
    def __init__(self, nc):
        self.nc = nc
        self.streams = {e: [] for e in ENGINES}

    def wait(self, eng, sem, n=None):
        v = sem.n if n is None else n
        if v <= 0:
            return
        self.streams[eng].append(lambda E, s=sem, v=v: E.wait_ge(s.h, v))

    _PTR_OPS = frozenset((
        "tensor_scalar", "tensor_single_scalar", "tensor_scalar_add",
        "tensor_scalar_sub", "tensor_scalar_mul", "tensor_scalar_max",
        "tensor_scalar_min", "scalar_tensor_tensor", "tensor_tensor_scan",
    ))

    def op(self, eng, meth, *a, inc=None, guard_pre=True, **kw):
        incs = []
        if inc is not None:
            incs = [inc] if isinstance(inc, tuple) else list(inc)
        for sm, k in incs:
            sm.n += k
        guard = meth in Prog._PTR_OPS and eng in ("vector", "gpsimd")

        def fn(E, meth=meth, a=a, kw=kw, incs=incs, guard=guard,
               guard_pre=guard_pre):
            if guard and guard_pre:
                E.drain()
            r = getattr(E, meth)(*a, **kw)
            if guard and not incs:
                E.drain()
            if incs:
                r.then_inc(incs[0][0].h, incs[0][1]) if not guard else None
                if guard:
                    E.drain(fusable=False).then_inc(incs[0][0].h, incs[0][1])
                for sm, k in incs[1:]:
                    E.drain(fusable=False).then_inc(sm.h, k)
            return r

        self.streams[eng].append(fn)

    def dma(self, eng, out, in_, sem, **kw):
        self.op(eng, "dma_start", inc=(sem, 16), out=out, in_=in_, **kw)

    def barrier(self, bsem):
        for eng in ENGINES:
            s = bsem[eng]
            s.n += 1
            if eng == "sync":
                self.streams[eng].append(lambda E, s=s: E.sem_inc(s.h, 1))
            elif eng == "tensor":
                self.streams[eng].append(lambda E, s=s: E.nop().then_inc(s.h, 1))
            else:
                self.streams[eng].append(
                    lambda E, s=s: E.drain(fusable=False).then_inc(s.h, 1))
        targets = [(bsem[eng], bsem[eng].n) for eng in ENGINES]
        for eng in ENGINES:
            for s, v in targets:
                self.wait(eng, s, v)

    def finalize(self):
        nc = self.nc
        with nc.Block() as block:
            @block.sync
            def _(E):
                for fn in self.streams["sync"]:
                    fn(E)

            @block.tensor
            def _(E):
                for fn in self.streams["tensor"]:
                    fn(E)

            @block.vector
            def _(E):
                for fn in self.streams["vector"]:
                    fn(E)

            @block.scalar
            def _(E):
                for fn in self.streams["scalar"]:
                    fn(E)

            @block.gpsimd
            def _(E):
                for fn in self.streams["gpsimd"]:
                    fn(E)


class Arena:

    def __init__(self, handle, cols):
        self.t = handle
        self.cols = cols
        self.off = 0

    def alloc(self, n):
        assert self.off + n <= self.cols, f"arena32 overflow {self.off}+{n}"
        v = self.t[:, self.off:self.off + n]
        self.off += n
        return v

    def alloc16(self, n):
        n2 = (n + 1) // 2
        assert self.off + n2 <= self.cols, f"arena16 overflow {self.off}+{n2}"
        v = self.t[:, self.off:self.off + n2].bitcast(BF16)
        self.off += n2
        return v

    def alloc8(self, n):
        n4 = (n + 3) // 4
        assert self.off + n4 <= self.cols, f"arena8 overflow {self.off}+{n4}"
        v = self.t[:, self.off:self.off + n4].bitcast(FP8)
        self.off += n4
        return v

    def mark(self):
        return self.off

    def reset(self, to=0):
        self.off = to


def build_program(dev=(), phases="A"):
    TH = T // 2
    NT = T // 512
    NTH = TH // 512
    TS = 1024 if T >= 1024 else T
    NTS = T // TS
    NH = TS // 512

    nc = bass.Bass("TRN2", target_bir_lowering=False, debug=False)
    dev = set(dev)

    def din(name, shape, dtype=F32):
        return nc.dram_tensor(name, list(shape), dtype, kind="ExternalInput").ap()

    x_in = din("x_tok", (T, D))
    xh_in = din("x_tok_half", (TH, D))
    ipw = din("ipw", (D, 2 * DIH))
    convw_i = din("convw", (DIH, 4))
    convb_i = din("convb", (DIH, 1))
    xpw_i = din("xpw", (DIH, 33))
    dtw_i = din("dtw", (DIH, 1))
    dtb_i = din("dtb", (DIH, 1))
    alog_i = din("alog", (DIH, S))
    dparam_i = din("dparam", (DIH, 1))
    cpwh_i = din("cpwh", (DIH, D))
    ident_i = din("ident", (128, 128))
    w1t_i = din("w1t", (D, 4 * D))
    b1c_i = din("b1c", (128, 32))
    w2t_i = din("w2t", (4 * D, D))
    b2c_i = din("b2c", (128, 8))
    mask_i = din("mask", (D, T // 2), BF16)
    aw1t_i = din("aw1t", (D, D))
    ab1_i = din("ab1", (128, 8))
    wwt_i = din("wwt", (D, 496))
    bw_i = din("bw", (1, 496))
    aw2t_i = din("aw2t", (D, D))
    ab2_i = din("ab2", (128, 8))
    gam_i = din("gam", (128, 8))
    bet_i = din("bet", (128, 8))
    sel_i = din("sel", (4, 128, 64), BF16)
    rsel_i = din("rsel", (8, 4, 64, 128), BF16)
    g1r_i = din("g1r", (4, 4, 8, 32))
    g2r_i = din("g2r", (4, 16, 16))
    g1i_i = din("g1i", (4, 4, 8, 32))
    g2i_i = din("g2i", (4, 16, 16))
    repd_i = din("repd", (8, 128), BF16)
    repg_i = din("repg", (16, 128), BF16)
    brc_i = din("brc", (128, 4))
    bic_i = din("bic", (128, 4))

    out_final = nc.dram_tensor("out_tok", [TH, D], F32, kind="ExternalOutput").ap()

    def scratch(name, shape, dtype):
        return nc.dram_tensor(name, list(shape), dtype).ap()

    xb_hbm = scratch("xb_silu_hbm", (DIH, T), BF16)
    z_hbm = scratch("z_silu_hbm", (DIH, T), BF16)
    cp_cc_in = scratch("cp_cc_in", (2, D, TH), F32)
    cp_cc_out = scratch("cp_cc_out", (D, TH), F32)
    xssm_cc_in = scratch("xssm_cc_in", (33, T), F32)
    xssm_cc_out = scratch("xssm_cc_out", (33, T), F32)
    xssm_bf = scratch("xssm_bf", (32, T), BF16)
    s1_hbm = scratch("s1_hbm", (D, TH), F32)
    s2_hbm = scratch("s2_hbm", (D, TH), F32)
    s3_hbm = scratch("s3_hbm", (D, TH), F32)
    hag_in = scratch("hag_in", (D, TH), BF16)
    hag_out = scratch("hag_out", (4, D, TH), BF16)

    dev_out = {}
    for name in dev:
        shp, dt = {
            "xb_silu": ((DIH, T), BF16), "z_silu": ((DIH, T), BF16),
            "xssm": ((33, T), F32), "cpo": ((D, TH), F32),
            "yg": ((DIH, T), BF16),
            "s2": ((D, TH), F32), "hC": ((D, TH), BF16),
            "wsm": ((TH, 496), F32), "conv": ((D, TH), F32),
            "s3": ((D, TH), F32),
            "s1": ((D, TH), F32),
        }[name]
        dev_out[name] = nc.dram_tensor("dev_" + name, list(shp), dt,
                                       kind="ExternalOutput").ap()

    p = Prog(nc)
    from contextlib import ExitStack
    stack = ExitStack()
    e = stack.enter_context

    sld = Sem(e(nc.semaphore("sld")))
    sg = Sem(e(nc.semaphore("sg")))
    spe = Sem(e(nc.semaphore("spe")))
    sve = Sem(e(nc.semaphore("sve")))
    ssc = Sem(e(nc.semaphore("ssc")))
    sgp = Sem(e(nc.semaphore("sgp")))
    scc = Sem(e(nc.semaphore("scc")))
    bsem = {eng: Sem(e(nc.semaphore("bar_" + eng))) for eng in ENGINES}

    ident_t = e(nc.sbuf_tensor("ident_t", [128, 128], F32))
    const_t = e(nc.sbuf_tensor("const_t", [128, 4], F32))
    for j, cv in enumerate((1e-5, -0.5, 1.0, -1.0)):
        nc.const_aps.register(F32, cv, const_t[:, j:j + 1])
    ones_row = e(nc.sbuf_tensor("ones_row", [1, 128], F32))
    ones_col = e(nc.sbuf_tensor("ones_col", [128, 1], BF16))
    convw_t = e(nc.sbuf_tensor("convw_t", [128, 32], F32))
    convb_t = e(nc.sbuf_tensor("convb_t", [128, 8], F32))
    dtw_t = e(nc.sbuf_tensor("dtw_t", [128, 8], F32))
    dtb_t = e(nc.sbuf_tensor("dtb_t", [128, 8], F32))
    aneg_t = e(nc.sbuf_tensor("aneg_t", [128, 8 * S], F32))
    dparam_t = e(nc.sbuf_tensor("dparam_t", [128, 8], F32))
    state_t = e(nc.sbuf_tensor("state_t", [128, 8 * S], BF16))
    xpw_t = e(nc.sbuf_tensor("xpw_t", [128, 8 * 33], BF16))

    AR = Arena(e(nc.sbuf_tensor("arena", [128, 52352], F32)), 52352)

    ps = [e(nc.psum_tensor(f"ps{i}", [128, 512], F32)) for i in range(8)]
    psm = [Sem(e(nc.semaphore(f"psm{i}"))) for i in range(8)]
    psf = [Sem(e(nc.semaphore(f"psf{i}"))) for i in range(8)]

    p.dma("sync", ident_t[:, :], ident_i[:, :], sld)
    p.wait("vector", sld)
    for j, cv in enumerate((1e-5, -0.5, 1.0, -1.0)):
        p.op("vector", "memset", const_t[:, j:j + 1], cv)
    p.op("vector", "memset", ones_row[:, :], 1.0)
    p.op("vector", "memset", ones_col[:, :], 1.0)
    p.op("vector", "memset", state_t[:, :], 0.0, inc=(sve, 1))

    ipw_t = AR.alloc16(8 * 2048)
    stage = AR.alloc(2048)
    for kb in range(8):
        p.wait("sync", ssc)
        p.dma("sync", stage[:, :], ipw[kb * 128:(kb + 1) * 128, :], sld)
        p.wait("scalar", sld)
        p.op("scalar", "copy", out=ipw_t[:, kb * 2048:(kb + 1) * 2048],
             in_=stage[:, :], inc=(ssc, 1))
    for db in range(8):
        rs = slice(db * 128, (db + 1) * 128)
        p.dma("sync", convw_t[:, db * 4:(db + 1) * 4], convw_i[rs, :], sld)
        p.dma("sync", convb_t[:, db:db + 1], convb_i[rs, :], sld)
        p.dma("sync", dtw_t[:, db:db + 1], dtw_i[rs, :], sld)
        p.dma("sync", dtb_t[:, db:db + 1], dtb_i[rs, :], sld)
        p.dma("sync", aneg_t[:, db * S:(db + 1) * S], alog_i[rs, :], sld)
        p.dma("sync", dparam_t[:, db:db + 1], dparam_i[rs, :], sld)
    p.wait("sync", ssc)
    for db in range(8):
        p.dma("sync", stage[:, db * 33:db * 33 + 33],
              xpw_i[db * 128:(db + 1) * 128, :], sld)
    p.wait("scalar", sld)
    p.op("scalar", "copy", out=xpw_t[:, :], in_=stage[:, 0:8 * 33])
    p.op("scalar", "activation", out=aneg_t[:, :], in_=aneg_t[:, :], func=AF.Exp,
         inc=(ssc, 1))
    p.wait("vector", ssc)
    p.op("vector", "tensor_scalar_mul", out=aneg_t[:, :], in0=aneg_t[:, :],
         scalar1=-1.0, inc=(sve, 1))
    a0_sc = ssc.n
    AR.reset(8 * 1024)

    xtok = AR.alloc(2 * 4096)
    xbbuf = AR.alloc(2 * 8 * 520)
    cvo = AR.alloc(2 * 512)
    xsst = AR.alloc(512)
    xfm = AR.alloc16(2 * 8 * 512)
    xbs = AR.alloc16(8 * 512)
    zst = AR.alloc16(8 * 512)

    def xbB(par, db):
        off = par * 8 * 520 + db * 520
        return xbbuf[:, off:off + 520]

    p.wait("gpsimd", sve)
    p.op("gpsimd", "memset", xbbuf[:, 0:2 * 8 * 520], 0.0, inc=(sgp, 1))
    halo_init = sgp.n
    p.wait("vector", sgp, halo_init)

    trans_done = {}
    conv_done = {}
    halo_c = {}
    store_done = {}
    xproj_done = {}
    xssm_ev = {}
    xssm_dma = {}
    silu_c = {}

    for it in range(NT):
        par = it % 2
        tsl = slice(it * 512, (it + 1) * 512)
        xt = xtok[:, par * 4096:(par + 1) * 4096]
        xfm_p = xfm[:, par * 8 * 512:(par + 1) * 8 * 512]

        if it == 0:
            p.wait("sync", ssc, a0_sc)
        if it >= 2:
            p.wait("sync", spe, trans_done[it - 2])
        for tt in range(4):
            p.dma("sync", xt[:, tt * 1024:(tt + 1) * 1024],
                  x_in[it * 512 + tt * 128: it * 512 + (tt + 1) * 128, :], sld)
        ld_all = sld.n

        p.wait("tensor", sld, ld_all)
        for db in range(8):
            bank = db % 2
            p.wait("tensor", psf[bank])
            for tt in range(4):
                p.op("tensor", "transpose", ps[bank][:, tt * 128:(tt + 1) * 128],
                     xt[:, tt * 1024 + db * 128: tt * 1024 + (db + 1) * 128],
                     ident_t[:, :], inc=(psm[bank], 1) if tt == 3 else None)
            p.wait("scalar", psm[bank])
            p.op("scalar", "copy", out=xfm_p[:, db * 512:(db + 1) * 512],
                 in_=ps[bank][:, :],
                 inc=[(psf[bank], 1)] + ([(ssc, 1)] if db == 7 else []))
        p.op("tensor", "nop", inc=(spe, 1))
        trans_done[it] = spe.n
        xfm_ready = ssc.n

        p.wait("tensor", ssc, xfm_ready)
        for mb in range(16):
            bank = 2 + (mb % 5)
            p.wait("tensor", psf[bank])
            for kb in range(8):
                p.op("tensor", "matmul", ps[bank][:, :],
                     ipw_t[:, kb * 2048 + mb * 128: kb * 2048 + (mb + 1) * 128],
                     xfm_p[:, kb * 512:(kb + 1) * 512],
                     start=(kb == 0), stop=(kb == 7),
                     inc=(psm[bank], 1) if kb == 7 else None)
            if mb < 8:
                db = mb
                if it >= 2 and mb == 0:
                    p.wait("vector", sve, conv_done[it - 2])
                p.wait("vector", psm[bank])
                p.op("vector", "tensor_copy", out=xbB(par, db)[:, 3:515],
                     in_=ps[bank][:, :], inc=[(psf[bank], 1), (sve, 1)])
                xb = xbB(par, db)
                if not NO_HALO:
                    p.wait("gpsimd", sve)
                    if it == 0 and db == 0:
                        p.wait("gpsimd", sgp, halo_init)
                    p.op("gpsimd", "tensor_copy", out=xbB(1 - par, db)[:, 0:3],
                         in_=xb[:, 512:515], inc=(sgp, 1))
                    halo_c[(it, db)] = sgp.n
                cv = cvo[:, (db % 2) * 512:(db % 2 + 1) * 512]
                prev = silu_c.get((it, db - 2)) or silu_c.get((it - 1, db + 6))
                if prev:
                    p.wait("vector", ssc, prev)
                hprev = halo_c.get((it - 1, db))
                if hprev and not NO_HALO:
                    p.wait("vector", sgp, hprev)
                if NO_CONV:
                    p.op("vector", "tensor_copy", out=cv[:, :], in_=xb[:, 3:515],
                         inc=(sve, 1))
                else:
                    p.op("vector", "tensor_scalar", out=cv[:, :], in0=xb[:, 0:512],
                         scalar1=convw_t[:, db * 4:db * 4 + 1], scalar2=None,
                         op0=AL.mult)
                    for k in (1, 2, 3):
                        p.op("vector", "scalar_tensor_tensor", out=cv[:, :],
                             in0=xb[:, k:k + 512],
                             scalar=convw_t[:, db * 4 + k:db * 4 + k + 1],
                             in1=cv[:, :], op0=AL.mult, op1=AL.add,
                             inc=(sve, 1) if k == 3 else None)
                if it >= 1 and db == 0:
                    p.wait("scalar", spe, xproj_done[it - 1])
                    p.wait("scalar", sld, store_done[it - 1])
                p.wait("scalar", sve)
                p.op("scalar", "activation", out=xbs[:, db * 512:(db + 1) * 512],
                     in_=(cv[:, :] if not RAW_SILU else xb[:, 3:515]),
                     func=AF.Silu, bias=convb_t[:, db:db + 1],
                     inc=(ssc, 1))
                silu_c[(it, db)] = ssc.n
            else:
                if it >= 1 and mb == 8:
                    p.wait("scalar", sld, store_done[it - 1])
                p.wait("scalar", psm[bank])
                p.op("scalar", "activation",
                     out=zst[:, (mb - 8) * 512:(mb - 7) * 512],
                     in_=ps[bank][:, :], func=AF.Silu,
                     inc=[(psf[bank], 1)] + ([(ssc, 1)] if mb == 15 else []))
        conv_done[it] = sve.n
        tile_sc = ssc.n

        p.wait("tensor", ssc, tile_sc)
        p.wait("tensor", psf[7])
        for db in range(8):
            p.op("tensor", "matmul", ps[7][0:33, :],
                 xpw_t[:, db * 33:(db + 1) * 33],
                 xbs[:, db * 512:(db + 1) * 512],
                 start=(db == 0), stop=(db == 7),
                 inc=(psm[7], 1) if db == 7 else None)
        p.op("tensor", "nop", inc=(spe, 1))
        xproj_done[it] = spe.n
        p.wait("vector", psm[7])
        if it >= 1:
            p.wait("vector", sld, xssm_dma[it - 1])
        p.op("vector", "tensor_copy", out=xsst[0:33, :],
             in_=ps[7][0:33, :], inc=[(psf[7], 1), (sve, 1)])
        xssm_ev[it] = sve.n
        p.wait("sync", sve, xssm_ev[it])
        p.dma("sync", xssm_cc_in[:, tsl], xsst[0:33, :], sld)
        xssm_dma[it] = sld.n

        p.wait("sync", ssc, tile_sc)
        p.dma("sync", xb_hbm[:, tsl].rearrange("(a p) t -> p a t", p=128),
              xbs[:, :].rearrange("p (a t) -> p a t", a=8), sld)
        p.dma("sync", z_hbm[:, tsl].rearrange("(a p) t -> p a t", p=128),
              zst[:, :].rearrange("p (a t) -> p a t", a=8), sld)
        store_done[it] = sld.n

    all_stores = sld.n
    if "xb_silu" in dev_out:
        p.wait("sync", sld, all_stores)
        p.dma("sync", dev_out["xb_silu"][:, :], xb_hbm[:, :], sld)
    if "z_silu" in dev_out:
        p.wait("sync", sld, all_stores)
        p.dma("sync", dev_out["z_silu"][:, :], z_hbm[:, :], sld)

    p.wait("gpsimd", sld, all_stores)
    p.op("gpsimd", "collective_compute", "AllReduce", AL.add,
         replica_groups=[[0, 1], [2, 3], [4, 5], [6, 7]],
         ins=[xssm_cc_in[:, :].opt()], outs=[xssm_cc_out[:, :].opt()],
         inc=(scc, 1))
    xssm_cc_done = scc.n
    if "xssm" in dev_out:
        p.wait("gpsimd", scc, xssm_cc_done)
        for rr in range(33):
            p.dma("gpsimd", dev_out["xssm"][rr:rr + 1, :],
                  xssm_cc_out[rr:rr + 1, :], sg)
        p.wait("gpsimd", sg)

    p.barrier(bsem)
    AR.reset(0)
    cpwh_t = AR.alloc16(8 * 1024)
    dtrow_bc = AR.alloc(TS)
    dt_all = AR.alloc(8 * TS)
    dtx_all = AR.alloc16(8 * TS)
    a_t = AR.alloc16(2 * TS)
    u_s = AR.alloc16(8 * TS)
    sp_tmp = AR.alloc(TS)
    h_t = AR.alloc16(2 * TS)
    ym_t = dtrow_bc[:, 0:TS // 2].bitcast(BF16)
    y_all = AR.alloc16(8 * TS)
    cp_st = AR.alloc(2 * 512)
    bbc = AR.alloc16(2 * TS)
    cbc = AR.alloc16(2 * TS)
    xbs_rd = AR.alloc16(2 * 8 * TS)
    zs_rd = AR.alloc16(8 * TS)

    xst32 = dt_all
    xst16 = dtx_all
    p.wait("sync", scc, xssm_cc_done)
    p.dma("sync", xst32[0:32, 0:T], xssm_cc_out[0:32, :], sld)
    p.wait("scalar", sld)
    p.op("scalar", "copy", out=xst16[0:32, 0:T], in_=xst32[0:32, 0:T],
         inc=(ssc, 1))
    p.wait("sync", ssc)
    p.dma("sync", xssm_bf[0:32, :], xst16[0:32, 0:T], sld)
    xbf_done = sld.n

    cstage = zs_rd[:, 0:2048].bitcast(F32)
    for kb in range(8):
        p.wait("sync", ssc)
        p.dma("sync", cstage[:, :], cpwh_i[kb * 128:(kb + 1) * 128, :], sld)
        p.wait("scalar", sld)
        p.op("scalar", "copy", out=cpwh_t[:, kb * 1024:(kb + 1) * 1024],
             in_=cstage[:, :], inc=(ssc, 1))
    cpw_ready = ssc.n

    grp_done = {}
    sub_done = {}
    state_c = {}
    cp_dma = {}
    xssm_ld = {}

    for ts_i in range(NTS):
        half = (ts_i * TS) // TH
        tsl = slice(ts_i * TS, (ts_i + 1) * TS)
        lsl = slice(ts_i * TS - half * TH, (ts_i + 1) * TS - half * TH)
        par = ts_i % 2
        xbr = xbs_rd[:, par * 8 * TS:(par + 1) * 8 * TS]

        if ts_i >= 2:
            p.wait("sync", spe, sub_done[ts_i - 2][1])
            p.wait("sync", sve, sub_done[ts_i - 2][0])
        p.wait("sync", sld, all_stores)
        p.dma("sync", xbr[:, :].rearrange("p (a t) -> p a t", a=8),
              xb_hbm[:, tsl].rearrange("(a p) t -> p a t", p=128), sld)
        if ts_i >= 1:
            p.wait("sync", sve, sub_done[ts_i - 1][0])
        p.wait("sync", scc, xssm_cc_done)
        p.dma("sync", dtrow_bc[:, :],
              xssm_cc_out[32:33, tsl].partition_broadcast(128)[:, 0, :], sld)
        ld_sub = sld.n

        p.wait("scalar", sld, ld_sub)
        if ts_i == 0:
            p.wait("vector", sld, xbf_done)
        if ts_i >= 1:
            p.wait("scalar", sve, sub_done[ts_i - 1][0])
        p.wait("scalar", ssc, cpw_ready)
        for db in range(8):
            dta = dt_all[:, db * TS:(db + 1) * TS]
            p.op("scalar", "activation", out=sp_tmp[:, :], in_=dtrow_bc[:, :],
                 func=AF.Abs, scale=dtw_t[:, db:db + 1], bias=dtb_t[:, db:db + 1])
            p.op("scalar", "activation", out=sp_tmp[:, :], in_=sp_tmp[:, :],
                 func=AF.Exp, scale=-1.0)
            p.op("scalar", "activation", out=sp_tmp[:, :], in_=sp_tmp[:, :],
                 func=AF.Ln, bias=1.0)
            p.op("scalar", "activation", out=dta[:, :], in_=dtrow_bc[:, :],
                 func=AF.Relu, scale=dtw_t[:, db:db + 1], bias=dtb_t[:, db:db + 1],
                 inc=(ssc, 1))
            p.wait("vector", ssc)
            p.op("vector", "tensor_add", out=dta[:, :], in0=dta[:, :],
                 in1=sp_tmp[:, :])
            p.op("vector", "tensor_mul", out=dtx_all[:, db * TS:(db + 1) * TS],
                 in0=dta[:, :], in1=xbr[:, db * TS:(db + 1) * TS], inc=(sve, 1))
            p.wait("scalar", sve)
        dt_sc = ssc.n

        p.wait("sync", ssc, cpw_ready)
        p.dma("sync", zs_rd[:, :].rearrange("p (a t) -> p a t", a=8),
              z_hbm[:, tsl].rearrange("(a p) t -> p a t", p=128), sld)
        z_ld = sld.n

        for s in range(S):
            spar = s % 2
            bb = bbc[:, spar * TS:(spar + 1) * TS]
            cb = cbc[:, spar * TS:(spar + 1) * TS]
            if (ts_i, (s - 2) * 8 + 7) in grp_done:
                p.wait("sync", sve, grp_done[(ts_i, (s - 2) * 8 + 7)])
            elif s < 2 and ts_i >= 1:
                p.wait("sync", sve, grp_done[(ts_i - 1, (S - 2 + s) * 8 + 7)])
            if ts_i == 0 and s == 0:
                p.wait("sync", sld, xbf_done)
            p.dma("sync", bb[:, :],
                  xssm_bf[s:s + 1, tsl].partition_broadcast(128)[:, 0, :], sld)
            p.dma("sync", cb[:, :],
                  xssm_bf[16 + s:17 + s, tsl].partition_broadcast(128)[:, 0, :],
                  sld)
            bc_ld = sld.n

            p.wait("vector", sld, bc_ld)
            for db in range(8):
                p.op("vector", "tensor_tensor",
                     out=u_s[:, db * TS:(db + 1) * TS],
                     in0=dtx_all[:, db * TS:(db + 1) * TS], in1=bb[:, :],
                     op=AL.mult)
            for db in range(8):
                g = s * 8 + db
                apar = g % 2
                at = a_t[:, apar * TS:(apar + 1) * TS]
                if (ts_i, g - 2) in grp_done:
                    p.wait("scalar", sve, grp_done[(ts_i, g - 2)])
                elif g < 2 and ts_i >= 1:
                    p.wait("scalar", sve, grp_done[(ts_i - 1, S * 8 - 2 + g)])
                p.wait("scalar", ssc, dt_sc)
                p.op("scalar", "activation", out=at[:, :],
                     in_=dt_all[:, db * TS:(db + 1) * TS], func=AF.Exp,
                     scale=aneg_t[:, db * S + s:db * S + s + 1], inc=(ssc, 1))
                a_rdy = ssc.n
                p.wait("vector", ssc, a_rdy)
                ht = h_t[:, (g % 2) * TS:(g % 2 + 1) * TS]
                if ts_i >= 1:
                    p.wait("vector", sgp, state_c[(ts_i - 1, g)])
                hbuf_prev = state_c.get((ts_i, g - 2)) or (
                    state_c.get((ts_i - 1, S * 8 - 2 + g)) if ts_i >= 1 else None)
                if hbuf_prev:
                    p.wait("vector", sgp, hbuf_prev)
                p.op("vector", "tensor_tensor_scan", out=ht[:, :],
                     data0=at[:, :], data1=u_s[:, db * TS:(db + 1) * TS],
                     initial=state_t[:, db * S + s:db * S + s + 1],
                     op0=AL.mult, op1=AL.add, inc=(sve, 1), guard_pre=False)
                p.wait("gpsimd", sve)
                p.op("gpsimd", "tensor_copy",
                     out=state_t[:, db * S + s:db * S + s + 1],
                     in_=ht[:, TS - 1:TS], inc=(sgp, 1))
                state_c[(ts_i, g)] = sgp.n
                p.op("vector", "tensor_tensor", out=ym_t[:, :],
                     in0=ht[:, :], in1=cb[:, :], op=AL.mult)
                ya = y_all[:, db * TS:(db + 1) * TS]
                if s == 0:
                    p.op("vector", "tensor_copy", out=ya[:, :], in_=ym_t[:, :],
                         inc=(sve, 1))
                else:
                    p.op("vector", "tensor_add", out=ya[:, :], in0=ya[:, :],
                         in1=ym_t[:, :], inc=(sve, 1))
                grp_done[(ts_i, g)] = sve.n

        p.wait("vector", sld, z_ld)
        for db in range(8):
            ya = y_all[:, db * TS:(db + 1) * TS]
            p.op("vector", "scalar_tensor_tensor", out=ya[:, :],
                 in0=xbr[:, db * TS:(db + 1) * TS],
                 scalar=dparam_t[:, db:db + 1], in1=ya[:, :],
                 op0=AL.mult, op1=AL.add)
            p.op("vector", "tensor_tensor", out=xbr[:, db * TS:(db + 1) * TS],
                 in0=ya[:, :], in1=zs_rd[:, db * TS:(db + 1) * TS], op=AL.mult,
                 inc=(sve, 1))
        yg_rdy = sve.n

        p.wait("tensor", sve, yg_rdy)
        chunk = ts_i // max(NTS // 2, 1)
        ccol = (ts_i % max(NTS // 2, 1)) * TS
        slot_dma = {}
        for mb in range(8):
            for hh in range(NH):
                u = mb * NH + hh
                bank = u % 4
                slot = u % 2
                p.wait("tensor", psf[bank])
                for kb in range(8):
                    p.op("tensor", "matmul", ps[bank][:, :],
                         cpwh_t[:, kb * 1024 + mb * 128: kb * 1024 + (mb + 1) * 128],
                         xbr[:, kb * TS + hh * 512: kb * TS + (hh + 1) * 512],
                         start=(kb == 0), stop=(kb == 7),
                         inc=(psm[bank], 1) if kb == 7 else None)
                prev = slot_dma.get(u - 2) or (cp_dma.get(ts_i - 1) if u < 2 else None)
                if prev:
                    p.wait("scalar", sld, prev)
                p.wait("scalar", psm[bank])
                p.op("scalar", "copy", out=cp_st[:, slot * 512:(slot + 1) * 512],
                     in_=ps[bank][:, :], inc=[(psf[bank], 1), (ssc, 1)])
                p.wait("sync", ssc)
                p.dma("sync",
                      cp_cc_in[chunk, mb * 128:(mb + 1) * 128,
                               ccol + hh * 512: ccol + (hh + 1) * 512],
                      cp_st[:, slot * 512:(slot + 1) * 512], sld)
                slot_dma[u] = sld.n
        p.op("tensor", "nop", inc=(spe, 1))
        sub_done[ts_i] = (sve.n, spe.n)
        cp_dma[ts_i] = sld.n
        if "yg" in dev_out:
            p.wait("gpsimd", sve, yg_rdy)
            p.dma("gpsimd",
                  dev_out["yg"][:, tsl].rearrange("(a p) t -> p a t", p=128),
                  xbr[:, :].rearrange("p (a t) -> p a t", a=8), sg)

    p.wait("gpsimd", sld, sld.n)
    p.op("gpsimd", "collective_compute", "ReduceScatter", AL.add,
         replica_groups=[[0, 1], [2, 3], [4, 5], [6, 7]],
         ins=[cp_cc_in[:, :, :].opt()], outs=[cp_cc_out[:, :].opt()],
         inc=(scc, 1))
    cp_cc_done = scc.n
    if "cpo" in dev_out:
        p.wait("gpsimd", scc, cp_cc_done)
        p.dma("gpsimd", dev_out["cpo"][:, :], cp_cc_out[:, :], sg)

    p.barrier(bsem)
    AR.reset(0)
    xtok3 = AR.alloc(2 * 4096)
    cp_rd = AR.alloc(2 * 8 * 512)
    mo_t = AR.alloc(2 * 8 * 512)
    s1out = AR.alloc(8 * 512)
    tokout = AR.alloc(2 * 1024)
    lnr = AR.alloc(2048)
    lnr1 = lnr[0:1, :]
    s1bf = AR.alloc16(8 * 512)
    sq16 = AR.alloc16(8 * 512)

    a3_done = {}
    tok_dma = {}
    for nt in range(NTH):
        par = nt % 2
        tsl = slice(nt * 512, (nt + 1) * 512)
        xt3 = xtok3[:, par * 4096:(par + 1) * 4096]
        cpr = cp_rd[:, par * 4096:(par + 1) * 4096]
        mo = mo_t[:, par * 4096:(par + 1) * 4096]

        p.wait("sync", scc, cp_cc_done)
        if nt >= 2:
            p.wait("sync", sve, a3_done[nt - 2][0])
            p.wait("sync", spe, a3_done[nt - 2][1])
        p.dma("sync", cpr[:, :].rearrange("p (a t) -> p a t", a=8),
              cp_cc_out[:, tsl].rearrange("(a p) t -> p a t", p=128), sld)
        for tt in range(4):
            p.dma("sync", xt3[:, tt * 1024:(tt + 1) * 1024],
                  xh_in[nt * 512 + tt * 128: nt * 512 + (tt + 1) * 128, :], sld)
        xld = sld.n

        p.wait("tensor", sld, xld)
        for mb in range(8):
            tbank = mb % 2
            p.wait("tensor", psf[tbank])
            for tt in range(4):
                p.op("tensor", "transpose",
                     ps[tbank][:, tt * 128:(tt + 1) * 128],
                     xt3[:, tt * 1024 + mb * 128: tt * 1024 + (mb + 1) * 128],
                     ident_t[:, :], inc=(psm[tbank], 1) if tt == 3 else None)
            p.wait("vector", psm[tbank])
            p.op("vector", "tensor_add", out=mo[:, mb * 512:(mb + 1) * 512],
                 in0=cpr[:, mb * 512:(mb + 1) * 512], in1=ps[tbank][:, :],
                 inc=(psf[tbank], 1))
            p.op("vector", "tensor_copy", out=s1bf[:, mb * 512:(mb + 1) * 512],
                 in_=mo[:, mb * 512:(mb + 1) * 512], inc=(sve, 1))
            p.wait("scalar", sve)
            p.op("scalar", "activation", out=sq16[:, mb * 512:(mb + 1) * 512],
                 in_=mo[:, mb * 512:(mb + 1) * 512], func=AF.Square, inc=(ssc, 1))
        p.op("tensor", "nop", inc=(spe, 1))
        sbf_rdy, sq_rdy = sve.n, ssc.n

        p.wait("tensor", sve, sbf_rdy)
        p.wait("tensor", ssc, sq_rdy)
        p.wait("tensor", psf[6])
        p.wait("tensor", psf[7])
        for kb in range(8):
            p.op("tensor", "matmul", ps[6][0:1, :], ones_col[:, :],
                 s1bf[:, kb * 512:(kb + 1) * 512], start=(kb == 0), stop=(kb == 7),
                 inc=(psm[6], 1) if kb == 7 else None)
        for kb in range(8):
            p.op("tensor", "matmul", ps[7][0:1, :], ones_col[:, :],
                 sq16[:, kb * 512:(kb + 1) * 512], start=(kb == 0), stop=(kb == 7),
                 inc=(psm[7], 1) if kb == 7 else None)
        p.wait("vector", psm[6])
        p.wait("vector", psm[7])
        if nt >= 1:
            p.wait("vector", sld, a3_done[nt - 1][2])
        p.op("vector", "tensor_scalar_mul", out=lnr1[:, 0:512],
             in0=ps[6][0:1, :], scalar1=1.0 / D, inc=(psf[6], 1))
        p.op("vector", "tensor_scalar_mul", out=lnr1[:, 512:1024],
             in0=ps[7][0:1, :], scalar1=1.0 / D, inc=(psf[7], 1))
        p.op("vector", "tensor_tensor", out=lnr1[:, 1024:1536],
             in0=lnr1[:, 0:512], in1=lnr1[:, 0:512], op=AL.mult)
        p.op("vector", "tensor_tensor", out=lnr1[:, 512:1024],
             in0=lnr1[:, 512:1024], in1=lnr1[:, 1024:1536], op=AL.subtract,
             inc=(sve, 1))
        p.wait("scalar", sve)
        p.op("scalar", "activation", out=lnr1[:, 512:1024],
             in_=lnr1[:, 512:1024], func=AF.Ln, bias=1e-5)
        p.op("scalar", "activation", out=lnr1[:, 512:1024],
             in_=lnr1[:, 512:1024], func=AF.Exp, scale=-0.5, inc=(ssc, 1))
        p.wait("vector", ssc)
        p.op("vector", "tensor_tensor", out=lnr1[:, 0:512], in0=lnr1[:, 0:512],
             in1=lnr1[:, 512:1024], op=AL.mult)
        p.op("vector", "tensor_scalar_mul", out=lnr1[:, 0:512],
             in0=lnr1[:, 0:512], scalar1=-1.0, inc=(sve, 1))
        ab_rdy = sve.n

        p.wait("tensor", sve, ab_rdy)
        p.wait("tensor", psf[2])
        p.op("tensor", "matmul", ps[2][:, :], ones_row[:, :],
             lnr1[0:1, 512:1024], start=True, stop=True, inc=(psm[2], 1))
        p.wait("tensor", psf[3])
        p.op("tensor", "matmul", ps[3][:, :], ones_row[:, :],
             lnr1[0:1, 0:512], start=True, stop=True, inc=(psm[3], 1))
        p.wait("vector", psm[2])
        p.wait("vector", psm[3])
        for db in range(8):
            p.op("vector", "tensor_tensor", out=s1out[:, db * 512:(db + 1) * 512],
                 in0=mo[:, db * 512:(db + 1) * 512], in1=ps[2][:, :],
                 op=AL.mult)
            p.op("vector", "tensor_add", out=s1out[:, db * 512:(db + 1) * 512],
                 in0=s1out[:, db * 512:(db + 1) * 512], in1=ps[3][:, :],
                 inc=[(sve, 1)] + ([(psf[2], 1), (psf[3], 1)] if db == 7 else []))
        s1_rdy = sve.n
        p.wait("sync", sve, s1_rdy)
        p.dma("sync", s1_hbm[:, tsl].rearrange("(a p) t -> p a t", p=128),
              s1out[:, :].rearrange("p (a t) -> p a t", a=8), sld)

        if phases == "A":
            p.wait("tensor", sve, s1_rdy)
            for tt in range(4):
                tb = tt % 2
                tko = tokout[:, tb * 1024:(tb + 1) * 1024]
                for db in range(8):
                    bk = 4 + (db // 4)
                    if db % 4 == 0:
                        p.wait("tensor", psf[bk])
                    p.op("tensor", "transpose",
                         ps[bk][:, (db % 4) * 128:(db % 4 + 1) * 128],
                         s1out[:, db * 512 + tt * 128: db * 512 + (tt + 1) * 128],
                         ident_t[:, :],
                         inc=(psm[bk], 1) if db % 4 == 3 else None)
                if (nt, tt - 2) in tok_dma:
                    p.wait("vector", sld, tok_dma[(nt, tt - 2)])
                elif tt < 2 and (nt - 1, 2 + tt) in tok_dma:
                    p.wait("vector", sld, tok_dma[(nt - 1, 2 + tt)])
                p.wait("vector", psm[4])
                p.op("vector", "tensor_copy", out=tko[:, 0:512], in_=ps[4][:, :],
                     inc=(psf[4], 1))
                p.wait("vector", psm[5])
                p.op("vector", "tensor_copy", out=tko[:, 512:1024], in_=ps[5][:, :],
                     inc=[(psf[5], 1), (sve, 1)])
                p.wait("sync", sve)
                p.dma("sync",
                      out_final[nt * 512 + tt * 128: nt * 512 + (tt + 1) * 128, :],
                      tko[:, :], sld)
                tok_dma[(nt, tt)] = sld.n
        a3_done[nt] = (sve.n, spe.n, sld.n)

    if "s1" in dev_out:
        p.wait("sync", sld)
        p.dma("sync", dev_out["s1"][:, :], s1_hbm[:, :], sld)

    def load_weight_cast(dst, src_dram, kblocks, cols, stage, dt_note=None):
        for kb in range(kblocks):
            p.wait("sync", ssc)
            p.dma("sync", stage[:, 0:cols], src_dram[kb * 128:(kb + 1) * 128, :],
                  sld)
            p.wait("scalar", sld)
            p.op("scalar", "copy", out=dst[:, kb * cols:(kb + 1) * cols],
                 in_=stage[:, 0:cols], inc=(ssc, 1))
        return ssc.n

    ln_state = {"lnr_guard": 0}

    def ln_tile(src, out, lnrv, sbfv, sqv, sbk, abk, gam=None, bet=None):
        p.wait("scalar", sve)
        for db in range(8):
            sl = slice(db * 512, (db + 1) * 512)
            p.op("vector", "tensor_copy", out=sbfv[:, sl], in_=src[:, sl],
                 inc=(sve, 1) if db == 7 else None)
            p.op("scalar", "activation", out=sqv[:, sl], in_=src[:, sl],
                 func=AF.Square, inc=(ssc, 1) if db == 7 else None)
        sbf_rdy, sq_rdy = sve.n, ssc.n
        p.wait("tensor", sve, sbf_rdy)
        p.wait("tensor", ssc, sq_rdy)
        p.wait("tensor", psf[sbk[0]])
        for kb in range(8):
            p.op("tensor", "matmul", ps[sbk[0]][0:1, :], ones_col[:, :],
                 sbfv[:, kb * 512:(kb + 1) * 512], start=(kb == 0), stop=(kb == 7),
                 inc=(psm[sbk[0]], 1) if kb == 7 else None)
        p.wait("tensor", psf[sbk[1]])
        for kb in range(8):
            p.op("tensor", "matmul", ps[sbk[1]][0:1, :], ones_col[:, :],
                 sqv[:, kb * 512:(kb + 1) * 512], start=(kb == 0), stop=(kb == 7),
                 inc=(psm[sbk[1]], 1) if kb == 7 else None)
        p.wait("vector", psm[sbk[0]])
        p.wait("vector", psm[sbk[1]])
        if ln_state["lnr_guard"]:
            p.wait("vector", spe, ln_state["lnr_guard"])
        p.op("vector", "tensor_scalar_mul", out=lnrv[:, 0:512],
             in0=ps[sbk[0]][0:1, :], scalar1=1.0 / D, inc=(psf[sbk[0]], 1))
        p.op("vector", "tensor_scalar_mul", out=lnrv[:, 512:1024],
             in0=ps[sbk[1]][0:1, :], scalar1=1.0 / D, inc=(psf[sbk[1]], 1))
        p.op("vector", "tensor_tensor", out=lnrv[:, 1024:1536],
             in0=lnrv[:, 0:512], in1=lnrv[:, 0:512], op=AL.mult)
        p.op("vector", "tensor_tensor", out=lnrv[:, 512:1024],
             in0=lnrv[:, 512:1024], in1=lnrv[:, 1024:1536], op=AL.subtract,
             inc=(sve, 1))
        p.wait("scalar", sve)
        p.op("scalar", "activation", out=lnrv[:, 512:1024],
             in_=lnrv[:, 512:1024], func=AF.Ln, bias=1e-5)
        p.op("scalar", "activation", out=lnrv[:, 512:1024],
             in_=lnrv[:, 512:1024], func=AF.Exp, scale=-0.5, inc=(ssc, 1))
        p.wait("vector", ssc)
        p.op("vector", "tensor_tensor", out=lnrv[:, 0:512], in0=lnrv[:, 0:512],
             in1=lnrv[:, 512:1024], op=AL.mult)
        p.op("vector", "tensor_scalar_mul", out=lnrv[:, 0:512],
             in0=lnrv[:, 0:512], scalar1=-1.0, inc=(sve, 1))
        ab_rdy = sve.n
        p.wait("tensor", sve, ab_rdy)
        p.wait("tensor", psf[abk])
        p.op("tensor", "matmul", ps[abk][:, :], ones_row[:, :],
             lnrv[0:1, 512:1024], start=True, stop=True, inc=(psm[abk], 1))
        p.wait("vector", psm[abk])
        for db in range(8):
            sl = slice(db * 512, (db + 1) * 512)
            p.op("vector", "tensor_tensor", out=out[:, sl], in0=src[:, sl],
                 in1=ps[abk][:, :], op=AL.mult,
                 inc=(psf[abk], 1) if db == 7 else None)
        p.wait("tensor", psf[abk])
        p.op("tensor", "matmul", ps[abk][:, :], ones_row[:, :],
             lnrv[0:1, 0:512], start=True, stop=True, inc=(psm[abk], 1))
        ln_state["lnr_guard"] = spe.n + 0
        p.op("tensor", "nop", inc=(spe, 1))
        ln_state["lnr_guard"] = spe.n
        p.wait("vector", psm[abk])
        for db in range(8):
            sl = slice(db * 512, (db + 1) * 512)
            p.op("vector", "tensor_add", out=out[:, sl], in0=out[:, sl],
                 in1=ps[abk][:, :],
                 inc=(psf[abk], 1) if db == 7 else None)
        if gam is not None:
            for db in range(8):
                sl = slice(db * 512, (db + 1) * 512)
                p.op("vector", "tensor_scalar", out=out[:, sl], in0=out[:, sl],
                     scalar1=gam[:, db:db + 1], scalar2=bet[:, db:db + 1],
                     op0=AL.mult, op1=AL.add,
                     inc=(sve, 1) if db == 7 else None)
        else:
            p.op("vector", "drain", fusable=False, inc=(sve, 1))
        return sve.n

    tok_dma2 = {}

    def exit_tile(src, nt, tokoutv):
        for tt in range(4):
            tb = tt % 2
            tko = tokoutv[:, tb * 1024:(tb + 1) * 1024]
            for db in range(8):
                bk = 4 + (db // 4)
                if db % 4 == 0:
                    p.wait("tensor", psf[bk])
                p.op("tensor", "transpose",
                     ps[bk][:, (db % 4) * 128:(db % 4 + 1) * 128],
                     src[:, db * 512 + tt * 128: db * 512 + (tt + 1) * 128],
                     ident_t[:, :], inc=(psm[bk], 1) if db % 4 == 3 else None)
            if (nt, tt - 2) in tok_dma2:
                p.wait("vector", sld, tok_dma2[(nt, tt - 2)])
            elif tt < 2 and (nt - 1, 2 + tt) in tok_dma2:
                p.wait("vector", sld, tok_dma2[(nt - 1, 2 + tt)])
            p.wait("vector", psm[4])
            p.op("vector", "tensor_copy", out=tko[:, 0:512], in_=ps[4][:, :],
                 inc=(psf[4], 1))
            p.wait("vector", psm[5])
            p.op("vector", "tensor_copy", out=tko[:, 512:1024], in_=ps[5][:, :],
                 inc=[(psf[5], 1), (sve, 1)])
            p.wait("sync", sve)
            p.dma("sync",
                  out_final[nt * 512 + tt * 128: nt * 512 + (tt + 1) * 128, :],
                  tko[:, :], sld)
            tok_dma2[(nt, tt)] = sld.n

    if phases != "A":
        p.barrier(bsem)
        AR.reset(0)
        w1f8 = AR.alloc8(8 * 4096)
        w2f8 = AR.alloc8(32 * 1024)
        b1q = AR.alloc(32)
        b2q = AR.alloc(8)
        bq8 = AR.alloc8(32)
        s1t = AR.alloc(2 * 8 * 512)
        stageB = s1t[:, 0:4096]
        s2t = AR.alloc(2 * 8 * 512)
        lnrB = AR.alloc(2048)
        xq8 = AR.alloc8(8 * 512)
        h8 = AR.alloc8(32 * 512)
        hbst = AR.alloc16(2 * 512)
        maskt = AR.alloc16(2 * 8 * 512)
        sbfB = AR.alloc16(8 * 512)
        sqB = AR.alloc16(8 * 512)
        ln_state["lnr_guard"] = 0

        load_weight_cast(w1f8, w1t_i, 8, 4096, stageB)
        load_weight_cast(w2f8, w2t_i, 32, 1024, stageB)
        p.wait("sync", ssc)
        p.dma("sync", stageB[:, 0:32], b1c_i[:, :], sld)
        p.dma("sync", stageB[:, 32:40], b2c_i[:, :], sld)
        p.wait("scalar", sld)
        p.op("scalar", "copy", out=bq8[:, 0:32], in_=stageB[:, 0:32])
        p.op("scalar", "copy", out=b1q[:, :], in_=bq8[:, 0:32])
        p.op("scalar", "copy", out=bq8[:, 0:8], in_=stageB[:, 32:40])
        p.op("scalar", "copy", out=b2q[:, :], in_=bq8[:, 0:8], inc=(ssc, 1))
        bq_rdy = ssc.n

        bdone = {}
        for nt in range(NTH):
            par = nt % 2
            tsl = slice(nt * 512, (nt + 1) * 512)
            s1v = s1t[:, par * 4096:(par + 1) * 4096]
            s2v = s2t[:, par * 4096:(par + 1) * 4096]
            mkv = maskt[:, par * 4096:(par + 1) * 4096]
            if nt == 0:
                p.wait("sync", ssc, bq_rdy)
            if nt >= 2:
                p.wait("sync", sve, bdone[nt - 2][0])
                p.wait("sync", spe, bdone[nt - 2][1])
            p.dma("sync", s1v[:, :].rearrange("p (a t) -> p a t", a=8),
                  s1_hbm[:, tsl].rearrange("(a p) t -> p a t", p=128), sld)
            p.dma("sync", mkv[:, :].rearrange("p (a t) -> p a t", a=8),
                  mask_i[:, tsl].rearrange("(a p) t -> p a t", p=128), sld)
            s1_ld = sld.n
            p.wait("vector", sld, s1_ld)
            if nt >= 1:
                p.wait("vector", spe, bdone[nt - 1][1])
            for db in range(8):
                p.op("vector", "tensor_copy",
                     out=xq8[:, db * 512:(db + 1) * 512],
                     in_=s1v[:, db * 512:(db + 1) * 512],
                     inc=(sve, 1) if db == 7 else None)
            xq_rdy = sve.n
            p.wait("tensor", sve, xq_rdy)
            p.wait("tensor", ssc, bq_rdy)
            for mb in range(32):
                bank = mb % 3
                p.wait("tensor", psf[bank])
                for kb in range(8):
                    p.op("tensor", "matmul", ps[bank][:, :],
                         w1f8[:, kb * 4096 + mb * 128: kb * 4096 + (mb + 1) * 128],
                         xq8[:, kb * 512:(kb + 1) * 512],
                         start=(kb == 0), stop=(kb == 7),
                         inc=(psm[bank], 1) if kb == 7 else None)
                hb = hbst[:, (mb % 2) * 512:(mb % 2 + 1) * 512]
                if mb >= 2:
                    p.wait("scalar", sve)
                p.wait("scalar", psm[bank])
                p.op("scalar", "activation", out=hb[:, :], in_=ps[bank][:, :],
                     func=AF.Relu, bias=b1q[:, mb:mb + 1],
                     inc=[(psf[bank], 1), (ssc, 1)])
                p.wait("vector", ssc)
                p.op("vector", "tensor_copy",
                     out=h8[:, mb * 512:(mb + 1) * 512], in_=hb[:, :],
                     inc=(sve, 1))
            h8_rdy = sve.n
            p.wait("tensor", sve, h8_rdy)
            for mb in range(8):
                bank = 3 + (mb % 2)
                p.wait("tensor", psf[bank])
                for kb in range(32):
                    p.op("tensor", "matmul", ps[bank][:, :],
                         w2f8[:, kb * 1024 + mb * 128: kb * 1024 + (mb + 1) * 128],
                         h8[:, kb * 512:(kb + 1) * 512],
                         start=(kb == 0), stop=(kb == 31),
                         inc=(psm[bank], 1) if kb == 31 else None)
                hb = hbst[:, (mb % 2) * 512:(mb % 2 + 1) * 512]
                if True:
                    p.wait("scalar", sve)
                p.wait("scalar", psm[bank])
                p.op("scalar", "activation", out=hb[:, :], in_=ps[bank][:, :],
                     func=AF.Identity, bias=b2q[:, mb:mb + 1],
                     inc=[(psf[bank], 1), (ssc, 1)])
                sl = slice(mb * 512, (mb + 1) * 512)
                p.wait("vector", ssc)
                p.op("vector", "tensor_tensor", out=hb[:, :], in0=hb[:, :],
                     in1=mkv[:, sl], op=AL.mult)
                p.op("vector", "tensor_add", out=s2v[:, sl], in0=s1v[:, sl],
                     in1=hb[:, :], inc=(sve, 1))
            s2_rdy = sve.n
            p.wait("vector", sve, s2_rdy)
            ln_tile(s2v, s2v, lnrB[0:1, :], sbfB, sqB, (5, 6), 7)
            p.op("tensor", "nop", inc=(spe, 1))
            bdone[nt] = (sve.n, spe.n)
            p.wait("sync", sve, bdone[nt][0])
            p.dma("sync", s2_hbm[:, tsl].rearrange("(a p) t -> p a t", p=128),
                  s2v[:, :].rearrange("p (a t) -> p a t", a=8), sld)
            bdone[nt] = (sve.n, spe.n, sld.n)
        if "s2" in dev_out:
            p.wait("sync", sld)
            p.dma("sync", dev_out["s2"][:, :], s2_hbm[:, :], sld)

        p.barrier(bsem)
        AR.reset(0)
        aw1b = AR.alloc16(8 * 1024)
        wwtb = AR.alloc16(8 * 496)
        aw2b = AR.alloc16(8 * 1024)
        ab1v = AR.alloc(8)
        ab2v = AR.alloc(8)
        gamv = AR.alloc(8)
        betv = AR.alloc(8)
        bwb = AR.alloc16(512)
        onesb = AR.alloc16(128)
        s2rt = AR.alloc(2 * 8 * 512)
        stageC = s2rt[:, 0:1024]
        hbf = AR.alloc16(2 * 8 * 512)

        load_weight_cast(aw1b, aw1t_i, 8, 1024, stageC)
        load_weight_cast(wwtb, wwt_i, 8, 496, stageC)
        load_weight_cast(aw2b, aw2t_i, 8, 1024, stageC)
        p.wait("sync", ssc)
        p.dma("sync", ab1v[:, :], ab1_i[:, :], sld)
        p.dma("sync", ab2v[:, :], ab2_i[:, :], sld)
        p.dma("sync", gamv[:, :], gam_i[:, :], sld)
        p.dma("sync", betv[:, :], bet_i[:, :], sld)
        p.dma("sync", stageC[0:1, 0:496], bw_i[:, :], sld)
        p.wait("scalar", sld)
        p.op("scalar", "copy", out=bwb[0:1, 0:496], in_=stageC[0:1, 0:496])
        p.wait("vector", sld)
        p.op("vector", "memset", onesb[0:1, :], 1.0)
        p.op("vector", "memset", bwb[0:1, 496:512], 0.0, inc=(sve, 1))
        p.op("scalar", "drain", fusable=False, inc=(ssc, 1))
        c1_const = (sve.n, ssc.n)

        xbfC = AR.alloc16(2 * 8 * 512)

        c1done = {}
        for nt in range(NTH):
            par = nt % 2
            tsl = slice(nt * 512, (nt + 1) * 512)
            s2v = s2rt[:, par * 4096:(par + 1) * 4096]
            hbv = hbf[:, par * 4096:(par + 1) * 4096]
            xbv = xbfC[:, par * 4096:(par + 1) * 4096]
            if nt == 0:
                p.wait("sync", ssc, c1_const[1])
                p.wait("sync", sve, c1_const[0])
            if nt >= 2:
                p.wait("sync", sve, c1done[nt - 2][0])
                p.wait("sync", sld, c1done[nt - 2][2])
            p.dma("sync", s2v[:, :].rearrange("p (a t) -> p a t", a=8),
                  s2_hbm[:, tsl].rearrange("(a p) t -> p a t", p=128), sld)
            sld_t = sld.n
            p.wait("vector", sld, sld_t)
            p.wait("vector", sve, c1_const[0])
            if nt >= 2:
                p.wait("vector", spe, c1done[nt - 2][1])
            for db in range(8):
                p.op("vector", "tensor_copy",
                     out=xbv[:, db * 512:(db + 1) * 512],
                     in_=s2v[:, db * 512:(db + 1) * 512],
                     inc=(sve, 1) if db == 7 else None)
            x_rdy = sve.n
            p.wait("tensor", sve, x_rdy)
            p.wait("tensor", ssc, c1_const[1])
            for mb in range(8):
                bank = mb % 2
                p.wait("tensor", psf[bank])
                for kb in range(8):
                    p.op("tensor", "matmul", ps[bank][:, :],
                         aw1b[:, kb * 1024 + mb * 128: kb * 1024 + (mb + 1) * 128],
                         xbv[:, kb * 512:(kb + 1) * 512],
                         start=(kb == 0), stop=(kb == 7),
                         inc=(psm[bank], 1) if kb == 7 else None)
                p.wait("scalar", psm[bank])
                p.op("scalar", "activation", out=hbv[:, mb * 512:(mb + 1) * 512],
                     in_=ps[bank][:, :], func=AF.Identity,
                     bias=ab1v[:, mb:mb + 1],
                     inc=[(psf[bank], 1)] + ([(ssc, 1)] if mb == 7 else []))
            h_rdy = ssc.n
            p.op("tensor", "nop", inc=(spe, 1))
            p.wait("sync", ssc, h_rdy)
            p.dma("sync", hag_in[:, tsl].rearrange("(a p) t -> p a t", p=128),
                  hbv[:, :].rearrange("p (a t) -> p a t", a=8), sld)
            if "hC" in dev_out:
                p.dma("sync", dev_out["hC"][:, tsl].rearrange("(a p) t -> p a t", p=128),
                      hbv[:, :].rearrange("p (a t) -> p a t", a=8), sld)
            c1done[nt] = (sve.n, spe.n, sld.n)

        p.wait("gpsimd", sld, c1done[NTH - 1][2])
        p.op("gpsimd", "collective_compute", "AllGather", AL.bypass,
             replica_groups=[[0, 2, 4, 6], [1, 3, 5, 7]],
             ins=[hag_in[:, :].opt()], outs=[hag_out[:, :, :].opt()],
             inc=(scc, 1))
        hag_done = scc.n

        sel_sb = AR.alloc16(4 * 64)
        wsmt = AR.alloc(512)
        wsegs = AR.alloc(32)
        wsmTb = AR.alloc16(4 * 128)
        wtapb = AR.alloc16(512)
        cacc = AR.alloc(8 * 512)
        caccb = AR.alloc16(8 * 512)
        ctmp = AR.alloc(512)
        hjt = AR.alloc16(4 * 512)
        sit = AR.alloc(8 * 512)
        s3t = AR.alloc(8 * 512)
        lnrC = AR.alloc(2048)
        sbfC = AR.alloc16(8 * 512)
        sqC = AR.alloc16(8 * 512)
        rsel_sb = AR.alloc16(32 * 128)

        p.wait("sync", scc, hag_done)
        for bb in range(4):
            p.dma("sync", sel_sb[:, bb * 64:(bb + 1) * 64], sel_i[bb, :, :], sld)
        for cb in range(8):
            for j in range(4):
                u = cb * 4 + j
                p.dma("sync", rsel_sb[0:64, u * 128:(u + 1) * 128],
                      rsel_i[cb, j, :, :], sld)
        sel_ld = sld.n

        p.op("scalar", "memzero", wsmTb[:, 384:512])
        p.op("scalar", "drain", fusable=False, inc=(ssc, 1))

        c2done = {}
        hj_dma = {}
        for nt in range(NTH):
            tsl = slice(nt * 512, (nt + 1) * 512)
            par = nt % 2
            s2v = s2rt[:, par * 4096:(par + 1) * 4096]
            hbv = hbf[:, par * 4096:(par + 1) * 4096]
            if nt >= 2:
                p.wait("sync", sve, c2done[nt - 2][0])
                p.wait("sync", spe, c2done[nt - 2][1])
            p.dma("sync", s2v[:, :].rearrange("p (a t) -> p a t", a=8),
                  s2_hbm[:, tsl].rearrange("(a p) t -> p a t", p=128), sld)
            p.dma("sync", hbv[:, :].rearrange("p (a t) -> p a t", a=8),
                  hag_in[:, tsl].rearrange("(a p) t -> p a t", p=128), sld)
            c2_ld = sld.n

            p.wait("tensor", sld, c2_ld)
            p.wait("tensor", sld, sel_ld)
            if nt >= 1:
                p.wait("tensor", spe, c2done[nt - 1][1])
            widths = (128, 128, 128, 112)
            for tc in range(4):
                lb = 2 + (tc % 2)
                p.wait("tensor", psf[lb])
                for kb in range(8):
                    p.op("tensor", "matmul", ps[lb][:, 0:496],
                         hbv[:, kb * 512 + tc * 128: kb * 512 + (tc + 1) * 128],
                         wwtb[:, kb * 496:(kb + 1) * 496],
                         start=(kb == 0), stop=False)
                p.op("tensor", "matmul", ps[lb][:, 0:496], onesb[0:1, 0:128],
                     bwb[0:1, 0:496], start=False, stop=True, inc=(psm[lb], 1))
                p.wait("vector", psm[lb])
                p.op("vector", "reduce_max", out=wsegs[:, 16:17],
                     in_=ps[lb][:, 0:496], axis=AX.X)
                p.op("vector", "tensor_scalar_mul", out=wsegs[:, 16:17],
                     in0=wsegs[:, 16:17], scalar1=-1.0, inc=(sve, 1))
                p.wait("scalar", sve)
                p.op("scalar", "activation", out=wsmt[:, 0:496],
                     in_=ps[lb][:, 0:496], func=AF.Exp,
                     bias=wsegs[:, 16:17], inc=(ssc, 1))
                p.wait("vector", ssc)
                p.op("vector", "tensor_reduce", out=wsegs[:, 0:16],
                     in_=wsmt[:, 0:496].rearrange("p (h k) -> p h k", k=31),
                     axis=AX.X, op=AL.add, inc=(psf[lb], 1))
                p.op("vector", "drain", fusable=False)
                p.op("vector", "reciprocal", out=wsegs[:, 0:16],
                     in_=wsegs[:, 0:16])
                for hd in range(16):
                    p.op("vector", "tensor_scalar",
                         out=wsmt[:, hd * 31:(hd + 1) * 31],
                         in0=wsmt[:, hd * 31:(hd + 1) * 31],
                         scalar1=wsegs[:, hd:hd + 1], scalar2=None,
                         op0=AL.mult, inc=(sve, 1) if hd == 15 else None)
                sm_rdy = sve.n
                p.wait("tensor", sve, sm_rdy)
                for bb in range(4):
                    tb = bb % 2
                    p.wait("tensor", psf[tb])
                    p.op("tensor", "transpose", ps[tb][0:widths[bb], 0:128],
                         wsmt[:, bb * 128: bb * 128 + widths[bb]],
                         ident_t[:, :], inc=(psm[tb], 1))
                    p.wait("scalar", psm[tb])
                    p.op("scalar", "copy",
                         out=wsmTb[0:widths[bb], bb * 128:(bb + 1) * 128],
                         in_=ps[tb][0:widths[bb], 0:128],
                         inc=[(psf[tb], 1)] + ([(ssc, 1)] if bb == 3 else []))
                wsmT_rdy = ssc.n
                p.wait("tensor", ssc, wsmT_rdy)
                p.wait("tensor", psf[4])
                for bb in range(4):
                    p.op("tensor", "matmul", ps[4][0:64, 0:128],
                         sel_sb[:, bb * 64:(bb + 1) * 64],
                         wsmTb[:, bb * 128:(bb + 1) * 128],
                         start=(bb == 0), stop=(bb == 3),
                         inc=(psm[4], 1) if bb == 3 else None)
                p.wait("scalar", psm[4])
                p.op("scalar", "copy", out=wtapb[0:64, tc * 128:(tc + 1) * 128],
                     in_=ps[4][0:64, 0:128], inc=[(psf[4], 1), (ssc, 1)])
                if "wsm" in dev_out:
                    p.wait("sync", sve)
                    p.dma("sync",
                          dev_out["wsm"][nt * 512 + tc * 128:
                                         nt * 512 + (tc + 1) * 128, 0:496],
                          wsmt[:, 0:496], sld)
            wtap_rdy = ssc.n

            p.wait("tensor", ssc, wtap_rdy)
            for cb in range(8):
                for j in range(4):
                    u = cb * 4 + j
                    slot = u % 4
                    prev = hj_dma.get((nt, u - 4)) or (
                        hj_dma.get((nt - 1, 28 + u)) if nt >= 1 else None)
                    if prev:
                        p.wait("sync", sve, prev[1])
                    p.wait("sync", scc, hag_done)
                    p.dma("sync", hjt[:, slot * 512:(slot + 1) * 512],
                          hag_out[j, cb * 128:(cb + 1) * 128, tsl], sld)
                    hj_dma[(nt, u)] = (sld.n, None)
                    cbank = 5 + (u % 2)
                    p.wait("tensor", psf[cbank])
                    p.op("tensor", "matmul", ps[cbank][:, :],
                         rsel_sb[0:64, u * 128:(u + 1) * 128],
                         wtapb[0:64, 0:512],
                         start=True, stop=True, inc=(psm[cbank], 1))
                    p.wait("vector", psm[cbank])
                    p.wait("vector", sld, hj_dma[(nt, u)][0])
                    if j == 0:
                        p.op("vector", "tensor_tensor",
                             out=cacc[:, cb * 512:(cb + 1) * 512],
                             in0=hjt[:, slot * 512:(slot + 1) * 512],
                             in1=ps[cbank][:, :], op=AL.mult,
                             inc=[(psf[cbank], 1), (sve, 1)])
                    else:
                        p.op("vector", "tensor_tensor", out=ctmp[:, :],
                             in0=hjt[:, slot * 512:(slot + 1) * 512],
                             in1=ps[cbank][:, :], op=AL.mult,
                             inc=(psf[cbank], 1))
                        p.op("vector", "tensor_add",
                             out=cacc[:, cb * 512:(cb + 1) * 512],
                             in0=cacc[:, cb * 512:(cb + 1) * 512],
                             in1=ctmp[:, :], inc=(sve, 1))
                    hj_dma[(nt, u)] = (hj_dma[(nt, u)][0], sve.n)
                p.op("vector", "tensor_copy",
                     out=caccb[:, cb * 512:(cb + 1) * 512],
                     in_=cacc[:, cb * 512:(cb + 1) * 512],
                     inc=(sve, 1) if cb == 7 else None)
            conv_rdy = sve.n
            if "conv" in dev_out:
                p.wait("sync", sve, conv_rdy)
                p.dma("sync",
                      dev_out["conv"][:, tsl].rearrange("(a p) t -> p a t", p=128),
                      cacc[:, :].rearrange("p (a t) -> p a t", a=8), sld)

            p.wait("tensor", sve, conv_rdy)
            for mb in range(8):
                bank = mb % 2
                p.wait("tensor", psf[bank])
                for kb in range(8):
                    p.op("tensor", "matmul", ps[bank][:, :],
                         aw2b[:, kb * 1024 + mb * 128: kb * 1024 + (mb + 1) * 128],
                         caccb[:, kb * 512:(kb + 1) * 512],
                         start=(kb == 0), stop=(kb == 7),
                         inc=(psm[bank], 1) if kb == 7 else None)
                sl = slice(mb * 512, (mb + 1) * 512)
                p.wait("scalar", psm[bank])
                p.op("scalar", "activation", out=sit[:, sl],
                     in_=ps[bank][:, :], func=AF.Identity,
                     bias=ab2v[:, mb:mb + 1], inc=[(psf[bank], 1), (ssc, 1)])
                p.wait("vector", ssc)
                p.op("vector", "tensor_add", out=sit[:, sl], in0=sit[:, sl],
                     in1=s2v[:, sl], inc=(sve, 1))
            si_rdy = sve.n
            p.wait("vector", sve, si_rdy)
            ln_tile(sit, sit, lnrC[0:1, :], sbfC, sqC, (2, 3), 4,
                    gam=gamv, bet=betv)
            for db in range(8):
                sl = slice(db * 512, (db + 1) * 512)
                p.op("vector", "tensor_add", out=s3t[:, sl], in0=s2v[:, sl],
                     in1=sit[:, sl], inc=(sve, 1) if db == 7 else None)
            ln_tile(s3t, s3t, lnrC[0:1, :], sbfC, sqC, (2, 3), 4)
            p.op("tensor", "nop", inc=(spe, 1))
            s3_rdy = sve.n
            p.wait("sync", sve, s3_rdy)
            p.dma("sync", s3_hbm[:, tsl].rearrange("(a p) t -> p a t", p=128),
                  s3t[:, :].rearrange("p (a t) -> p a t", a=8), sld)
            c2done[nt] = (sve.n, spe.n, sld.n)
        if "s3" in dev_out:
            p.wait("sync", sld)
            p.dma("sync", dev_out["s3"][:, :], s3_hbm[:, :], sld)

        p.barrier(bsem)
        AR.reset(0)
        repd_sb = AR.alloc16(128)
        repg_sb = AR.alloc16(128)
        acb = AR.alloc16(32)
        bcb = AR.alloc16(16)
        arep = AR.alloc(32)
        brep = AR.alloc(16)
        aexp = AR.alloc(512)
        bexp = AR.alloc(512)
        wtmp = AR.alloc(512)
        wacc = AR.alloc(512)
        wrtb = AR.alloc16(4 * 512)
        witb = AR.alloc16(4 * 512)
        winegb = AR.alloc16(4 * 512)
        brcv = AR.alloc(4)
        bicv = AR.alloc(4)
        stageD = AR.alloc(64)
        s3rt = AR.alloc(2 * 8 * 512)
        x3b = AR.alloc16(8 * 512)
        y4t = AR.alloc(8 * 512)
        lnrD = AR.alloc(2048)
        sbfD = AR.alloc16(8 * 512)
        sqD = AR.alloc16(8 * 512)
        tokoutD = AR.alloc(2 * 1024)

        p.wait("sync", sld)
        p.dma("sync", repd_sb[0:8, :], repd_i[:, :], sld)
        p.dma("sync", repg_sb[0:16, :], repg_i[:, :], sld)
        p.dma("sync", brcv[:, :], brc_i[:, :], sld)
        p.dma("sync", bicv[:, :], bic_i[:, :], sld)
        rep_ld = sld.n

        for wi, (g1d, g2d, outb) in enumerate(
                ((g1r_i, g2r_i, wrtb), (g1i_i, g2i_i, witb))):
            for kb in range(4):
                for c in range(4):
                    p.wait("sync", ssc)
                    p.wait("sync", sve)
                    p.dma("sync", stageD[0:8, 0:32], g1d[c, kb, :, :], sld)
                    p.dma("sync", stageD[0:16, 32:48], g2d[c, :, :], sld)
                    p.wait("scalar", sld)
                    p.wait("scalar", psm[0])
                    p.wait("scalar", psm[1])
                    p.op("scalar", "copy", out=acb[0:8, :],
                         in_=stageD[0:8, 0:32])
                    p.op("scalar", "copy", out=bcb[0:16, :],
                         in_=stageD[0:16, 32:48], inc=(ssc, 1))
                    p.wait("tensor", ssc)
                    p.wait("tensor", sld, rep_ld)
                    p.wait("tensor", psf[0])
                    p.op("tensor", "matmul", ps[0][:, 0:32], repd_sb[0:8, :],
                         acb[0:8, :], start=True, stop=True, inc=(psm[0], 1))
                    p.wait("tensor", psf[1])
                    p.op("tensor", "matmul", ps[1][:, 0:16], repg_sb[0:16, :],
                         bcb[0:16, :], start=True, stop=True, inc=(psm[1], 1))
                    p.wait("vector", psm[0])
                    p.wait("vector", psm[1])
                    p.op("vector", "tensor_copy", out=arep[:, :],
                         in_=ps[0][:, 0:32], inc=(psf[0], 1))
                    p.op("vector", "tensor_copy", out=brep[:, :],
                         in_=ps[1][:, 0:16], inc=(psf[1], 1))
                    p.op("vector", "tensor_copy",
                         out=aexp[:, :].rearrange("p (b e) -> p b e", e=16),
                         in_=arep[:, :].unsqueeze(2).broadcast_to([128, 32, 16]))
                    p.op("vector", "tensor_copy",
                         out=bexp[:, :].rearrange("p (b e) -> p b e", e=16),
                         in_=brep[:, :].unsqueeze(1).broadcast_to([128, 32, 16]))
                    if c == 0:
                        p.op("vector", "tensor_tensor", out=wacc[:, :],
                             in0=aexp[:, :], in1=bexp[:, :], op=AL.mult,
                             inc=(sve, 1))
                    else:
                        p.op("vector", "tensor_tensor", out=wtmp[:, :],
                             in0=aexp[:, :], in1=bexp[:, :], op=AL.mult)
                        p.op("vector", "tensor_add", out=wacc[:, :],
                             in0=wacc[:, :], in1=wtmp[:, :], inc=(sve, 1))
                p.op("vector", "tensor_copy",
                     out=outb[:, kb * 512:(kb + 1) * 512], in_=wacc[:, :],
                     inc=(sve, 1))
        p.op("vector", "tensor_scalar_mul", out=winegb[:, :], in0=witb[:, :],
             scalar1=-1.0, inc=(sve, 1))
        wbuild_done = sve.n

        ddone = {}
        for nt in range(NTH):
            tsl = slice(nt * 512, (nt + 1) * 512)
            par = nt % 2
            s3v = s3rt[:, par * 4096:(par + 1) * 4096]
            if nt >= 2:
                p.wait("sync", sve, ddone[nt - 2][0])
            p.dma("sync", s3v[:, :].rearrange("p (a t) -> p a t", a=8),
                  s3_hbm[:, tsl].rearrange("(a p) t -> p a t", p=128), sld)
            s3_ld = sld.n
            p.wait("vector", sld, s3_ld)
            p.wait("vector", sve, wbuild_done)
            if nt >= 1:
                p.wait("vector", spe, ddone[nt - 1][1])
            for db in range(8):
                p.op("vector", "tensor_copy",
                     out=x3b[:, db * 512:(db + 1) * 512],
                     in_=s3v[:, db * 512:(db + 1) * 512],
                     inc=(sve, 1) if db == 7 else None)
            x3_rdy = sve.n
            p.wait("tensor", sve, x3_rdy)
            for part in range(2):
                for mb in range(4):
                    blk = part * 4 + mb
                    bank = blk % 4
                    p.wait("tensor", psf[bank])
                    wa = wrtb if part == 0 else witb
                    wb2 = winegb if part == 0 else wrtb
                    for kb in range(4):
                        p.op("tensor", "matmul", ps[bank][:, :],
                             wa[:, kb * 512 + mb * 128: kb * 512 + (mb + 1) * 128],
                             x3b[:, kb * 512:(kb + 1) * 512],
                             start=(kb == 0), stop=False)
                    for kb in range(4):
                        p.op("tensor", "matmul", ps[bank][:, :],
                             wb2[:, kb * 512 + mb * 128: kb * 512 + (mb + 1) * 128],
                             x3b[:, (4 + kb) * 512:(5 + kb) * 512],
                             start=False, stop=(kb == 3),
                             inc=(psm[bank], 1) if kb == 3 else None)
                    bv = brcv if part == 0 else bicv
                    sl = slice(blk * 512, (blk + 1) * 512)
                    p.wait("scalar", psm[bank])
                    p.op("scalar", "activation", out=y4t[:, sl],
                         in_=ps[bank][:, :], func=AF.Identity,
                         bias=bv[:, mb:mb + 1], inc=[(psf[bank], 1), (ssc, 1)])
                    p.wait("vector", ssc)
                    p.op("vector", "tensor_add", out=y4t[:, sl],
                         in0=y4t[:, sl], in1=s3v[:, sl], inc=(sve, 1))
            y4_rdy = sve.n
            p.wait("vector", sve, y4_rdy)
            ln_tile(y4t, y4t, lnrD[0:1, :], sbfD, sqD, (2, 3), 6)
            p.op("tensor", "nop", inc=(spe, 1))
            exit_tile(y4t, nt, tokoutD)
            ddone[nt] = (sve.n, spe.n)

    p.wait("sync", sld)
    p.wait("gpsimd", sg)
    p.barrier(bsem)

    p.finalize()
    stack.close()
    return nc



def _prep_inputs(inputs):
    f = np.ascontiguousarray
    x = np.asarray(inputs["L0_attn_x"])
    ipw_full = np.asarray(inputs["L0_attn_in_proj_w"])
    convw = np.asarray(inputs["L0_attn_conv_w"])
    convb = np.asarray(inputs["L0_attn_conv_b"])
    xpw = np.asarray(inputs["L0_attn_x_proj_w"])
    dtw = np.asarray(inputs["L0_attn_dt_proj_w"])
    dtb = np.asarray(inputs["L0_attn_dt_proj_b"])
    alog = np.asarray(inputs["L0_attn_A_log"])
    dparam = np.asarray(inputs["L0_attn_D_param"])
    cpw = np.asarray(inputs["L0_attn_c_proj_w"])
    ident = np.eye(128, dtype=np.float32)

    W1 = np.asarray(inputs["L0_ffn_W1"])
    b1 = np.asarray(inputs["L0_ffn_b1"])
    W2 = np.asarray(inputs["L0_ffn_W2"])
    b2 = np.asarray(inputs["L0_ffn_b2"])
    w1t = f(W1.T)
    w2t = f(W2.T)
    b1c = f(b1.reshape(32, 128).T)
    b2c = f(b2.reshape(8, 128).T)
    import jax
    with jax.default_device(jax.devices("cpu")[0]):
        keep = np.asarray(jax.random.bernoulli(
            jax.random.key(42, impl="threefry2x32"), 0.8, (B, T, D)))
    maskq = np.where(keep, np.float32(1.25), np.float32(0.0)).astype(BFNP)

    aw1 = np.asarray(inputs["L1_attn_w1"])
    ab1f = np.asarray(inputs["L1_attn_b1"])
    ww = np.asarray(inputs["L1_attn_ww"])
    bwf = np.asarray(inputs["L1_attn_bw"])
    aw2 = np.asarray(inputs["L1_attn_w2"])
    ab2f = np.asarray(inputs["L1_attn_b2"])
    gamf = np.asarray(inputs["L1_attn_gamma"])
    betf = np.asarray(inputs["L1_attn_beta"])
    aw1t = f(aw1.T)
    wwt = f(ww.T)
    aw2t = f(aw2.T)
    ab1 = f(ab1f.reshape(8, 128).T)
    ab2 = f(ab2f.reshape(8, 128).T)
    gamr = f(gamf.reshape(8, 128).T)
    betr = f(betf.reshape(8, 128).T)
    bwv = f(bwf[None, :])
    sel_by_core = []
    for bb_ in range(4):
        selm = np.zeros((512, 64), np.float32)
        for hd in range(16):
            for j in range(4):
                if j <= bb_:
                    col = hd * 31 + (30 - bb_ + j)
                    selm[col, hd * 4 + j] = 1.0
        sel_by_core.append(np.ascontiguousarray(
            selm.reshape(4, 128, 64).astype(BFNP)))
    rsel = np.zeros((8, 4, 64, 128), np.float32)
    for cb in range(8):
        for j in range(4):
            for hl in range(2):
                hd = 2 * cb + hl
                rsel[cb, j, hd * 4 + j, hl * 64:(hl + 1) * 64] = 1.0
    rsel = np.ascontiguousarray(rsel.astype(BFNP))

    G1R = np.asarray(inputs["L1_ffn_G1_R"])
    G2R = np.asarray(inputs["L1_ffn_G2_R"])
    G1I = np.asarray(inputs["L1_ffn_G1_I"])
    G2I = np.asarray(inputs["L1_ffn_G2_I"])
    def g1_lay(G):
        a = G[0].transpose(1, 2, 0)
        return f(a.reshape(4, 4, 8, 32))
    def g2_lay(G):
        return f(G[:, :, 0, :].transpose(0, 2, 1))
    g1r, g1i = g1_lay(G1R), g1_lay(G1I)
    g2r, g2i = g2_lay(G2R), g2_lay(G2I)
    repd = np.zeros((8, 128), np.float32)
    for dl in range(8):
        repd[dl, dl * 16:(dl + 1) * 16] = 1.0
    repd = np.ascontiguousarray(repd.astype(BFNP))
    repg = np.zeros((16, 128), np.float32)
    for g_ in range(16):
        repg[g_, g_::16] = 1.0
    repg = np.ascontiguousarray(repg.astype(BFNP))
    brf = np.asarray(inputs["L1_ffn_bias_real"])
    bif = np.asarray(inputs["L1_ffn_bias_imag"])
    brc = f(brf.reshape(4, 128).T)
    bic = f(bif.reshape(4, 128).T)

    in_maps = []
    for core in range(8):
        b, h = core // 2, core % 2
        hs = slice(h * DIH, (h + 1) * DIH)
        m = {
            "x_tok": f(x[b]),
            "x_tok_half": f(x[b, h * (T // 2):(h + 1) * (T // 2)]),
            "ipw": f(np.concatenate(
                [ipw_full[:, h * DIH:(h + 1) * DIH],
                 ipw_full[:, DI + h * DIH: DI + (h + 1) * DIH]], axis=1)),
            "convw": f(convw[hs, 0, :]),
            "convb": f(convb[hs, None]),
            "xpw": f(xpw[hs, :]),
            "dtw": f(dtw[0, hs, None]),
            "dtb": f(dtb[hs, None]),
            "alog": f(alog[hs, :]),
            "dparam": f(dparam[hs, None]),
            "cpwh": f(cpw[hs, :]),
            "ident": ident,
            "w1t": w1t,
            "b1c": b1c,
            "w2t": w2t,
            "b2c": b2c,
            "mask": f(maskq[b, h * (T // 2):(h + 1) * (T // 2), :].T),
            "aw1t": aw1t,
            "ab1": ab1,
            "wwt": wwt,
            "bw": bwv,
            "aw2t": aw2t,
            "ab2": ab2,
            "gam": gamr,
            "bet": betr,
            "sel": sel_by_core[b],
            "rsel": rsel,
            "g1r": g1r, "g2r": g2r, "g1i": g1i, "g2i": g2i,
            "repd": repd, "repg": repg,
            "brc": brc, "bic": bic,
        }
        in_maps.append(m)
    return in_maps


_CACHE = {}


def kernel(**inputs):
    key = "full"
    if key not in _CACHE:
        _CACHE[key] = build_program(phases="ABCD")
    nc = _CACHE[key]
    in_maps = _prep_inputs(inputs)
    res = run_bass_kernel_spmd(nc, in_maps, core_ids=list(range(8)))
    out = np.empty((B, T, D), dtype=np.float32)
    for core in range(8):
        b, h = core // 2, core % 2
        out[b, h * (T // 2):(h + 1) * (T // 2)] = res.results[core]["out_tok"]
    return out
